# version 32
# baseline (speedup 1.0000x reference)
import math
import numpy as np
import ml_dtypes
import concourse.bass as bass
import concourse.mybir as mybir
from concourse.bass_utils import run_bass_kernel_spmd

F32 = mybir.dt.float32
BF16 = mybir.dt.bfloat16
AF = mybir.ActivationFunctionType
ALU = mybir.AluOpType
AX = mybir.AxisListType
BF = ml_dtypes.bfloat16

LN_EPS = 1e-5


class Plan:
    ENGS = ("sync", "tensor", "vector", "scalar", "gpsimd")

    def __init__(self, nc):
        self.nc = nc
        self.plans = {e: [] for e in self.ENGS}
        self.sems = {}

    def sem(self, name):
        h = self.nc.alloc_semaphore(name)
        self.sems[name] = [h, 0]
        return name

    def cnt(self, name):
        return self.sems[name][1]

    def op(self, eng, fn, wait=(), inc=None, incn=1, lowp=None):
        tgt = None
        if inc is not None:
            self.sems[inc][1] += incn
            tgt = self.sems[inc][1]
        sems = self.sems
        nc = self.nc
        wait = tuple((s, v) for (s, v) in wait if v and v > 0)

        def thunk(e, fn=fn, wait=wait, inc=inc, incn=incn, lowp=lowp):
            for (s, v) in wait:
                e.wait_ge(sems[s][0], v)
            if lowp is not None:
                with nc.allow_low_precision(reason=lowp):
                    ins = fn(e)
            else:
                ins = fn(e)
            if inc is not None:
                ins.then_inc(sems[inc][0], incn)

        self.plans[eng].append(thunk)
        return tgt

    def drain(self, eng):
        self.plans[eng].append(lambda e: e.drain())


def build_program(SEQ, DM, H):
    assert SEQ == DM
    NCH = DM // 128
    NT = SEQ // 128
    SLW = min(512, SEQ)
    NSL = SEQ // SLW
    STW = min(512, DM)
    NST = DM // STW
    TPG = min(4, NT)
    NG = NT // TPG
    GQ = TPG * 128
    NPS = DM // 512 if DM >= 512 else 1
    PSW = min(512, DM)
    C_ATTN = 1.0 / math.sqrt(128.0)

    nc = bass.Bass("TRN2", target_bir_lowering=False)
    p = Plan(nc)

    x_h = nc.dram_tensor("x", [SEQ, DM], F32, kind="ExternalInput")
    wq_h = nc.dram_tensor("wq", [H, DM, 128], F32, kind="ExternalInput")
    wk_h = nc.dram_tensor("wk", [H, DM, 128], F32, kind="ExternalInput")
    wv_h = nc.dram_tensor("wv", [H, DM, 128], F32, kind="ExternalInput")
    wo_h = nc.dram_tensor("wo", [H, 128, DM], F32, kind="ExternalInput")
    bqkvT_h = nc.dram_tensor("bqkvT", [128, 3 * H], F32, kind="ExternalInput")
    bo_h = nc.dram_tensor("bo", [DM], F32, kind="ExternalInput")
    fw_h = nc.dram_tensor("fw", [DM, DM], F32, kind="ExternalInput")
    fb_h = nc.dram_tensor("fb", [DM], F32, kind="ExternalInput")
    eye_h = nc.dram_tensor("eye", [128, 128], BF16, kind="ExternalInput")
    tri_h = nc.dram_tensor("tri", [128, 128], BF16, kind="ExternalInput")
    onec_h = nc.dram_tensor("onec", [128, 1], BF16, kind="ExternalInput")
    oner_h = nc.dram_tensor("oner", [1, 128], BF16, kind="ExternalInput")
    onem_h = nc.dram_tensor("onem", [128, 128], BF16, kind="ExternalInput")
    epsb_h = nc.dram_tensor("epsb", [128, 1], F32, kind="ExternalInput")
    lncb_h = nc.dram_tensor("lncb", [1, 1], F32, kind="ExternalInput")
    out_h = nc.dram_tensor("out", [SEQ, DM], F32, kind="ExternalOutput")
    zs_h = nc.dram_tensor("zs", [NT, H, 128, 128], BF16)

    A = nc.alloc_sbuf_tensor("arena_a", [128, NCH * SEQ], BF16)
    Bz = nc.alloc_sbuf_tensor("arena_b", [128, H * DM], BF16)
    wbuf = nc.alloc_sbuf_tensor("wbuf", [128, 4 * NCH * 128], BF16)
    qt_sb = nc.alloc_sbuf_tensor("qt", [128, SEQ], BF16)
    kt_sb = nc.alloc_sbuf_tensor("kt", [128, SEQ], BF16)
    v_sb = nc.alloc_sbuf_tensor("v", [128, SEQ], BF16)
    ptil = nc.alloc_sbuf_tensor("ptil", [128, max(NT * GQ, 4 * DM)], BF16)
    C2 = nc.alloc_sbuf_tensor("c2", [128, max(6 * DM, 4 * GQ + SEQ + 4 * DM)], BF16)
    eye_sb = nc.alloc_sbuf_tensor("eye_sb", [128, 128], BF16)
    tri_sb = nc.alloc_sbuf_tensor("tri_sb", [128, 128], BF16)
    onec_sb = nc.alloc_sbuf_tensor("onec_sb", [128, 1], BF16)
    oner_sb = nc.alloc_sbuf_tensor("oner_sb", [1, 128], BF16)
    bqkvT_sb = nc.alloc_sbuf_tensor("bqkvT_sb", [128, 3 * H], F32)
    bo_sb = nc.alloc_sbuf_tensor("bo_sb", [1, DM], BF16)
    fb_sb = nc.alloc_sbuf_tensor("fb_sb", [1, DM], BF16)
    st6 = [nc.alloc_sbuf_tensor(f"st6_{i}", [128, 6 * NST], F32) for i in range(3)]
    mv2 = [nc.alloc_sbuf_tensor(f"mv2_{i}", [128, 2], F32) for i in range(3)]
    rstd = [nc.alloc_sbuf_tensor(f"rstd_{i}", [128, 1], F32) for i in range(3)]
    corr = nc.alloc_sbuf_tensor("corr", [128, 3], F32)
    onem_sb = nc.alloc_sbuf_tensor("onem_sb", [128, 128], BF16)
    mbias = nc.alloc_sbuf_tensor("mbias", [128, 1], F32)
    srow = nc.alloc_sbuf_tensor("srow", [1, 24], F32)
    nm_bf = nc.alloc_sbuf_tensor("nm_bf", [1, 1], BF16)
    asum = nc.alloc_sbuf_tensor("asum", [128, 1], F32)
    twoal = nc.alloc_sbuf_tensor("twoal", [128, 1], F32)
    negal = nc.alloc_sbuf_tensor("negal", [128, 1], F32)
    epsb_sb = nc.alloc_sbuf_tensor("epsb_sb", [128, 1], F32)
    lncb_sb = nc.alloc_sbuf_tensor("lncb_sb", [1, 1], F32)

    ln1t3 = A.rearrange("p (c s) -> p c s", c=NCH)
    bwt3 = ln1t3
    wslot = [wbuf[:, s * NCH * 128:(s + 1) * NCH * 128] for s in range(4)]
    x3v = wbuf.bitcast(F32)
    x3buf = x3v[:, 0:DM]
    s2buf = x3v[:, DM:2 * DM]
    pslot = [ptil[:, j * GQ:(j + 1) * GQ] for j in range(NT)]
    vstage = ptil[:, 0:SEQ]
    xinv = ptil.bitcast(F32)
    xin = [xinv[:, i * DM:(i + 1) * DM] for i in range(2)]
    c2f = C2.bitcast(F32)
    xa = [c2f[:, i * DM:(i + 1) * DM] for i in range(3)] + \
        [ptil.bitcast(F32)[:, SEQ:SEQ + DM]]
    zstage = [C2[:, i * GQ:(i + 1) * GQ] for i in range(2)]
    rbc = [C2[:, (2 + i) * GQ:(3 + i) * GQ] for i in range(2)]
    sqb = C2[:, 4 * GQ:4 * GQ + SEQ]
    fw_off = (4 * GQ + SEQ) // 2
    fwst = [c2f[:, fw_off + i * DM:fw_off + (i + 1) * DM] for i in range(2)]
    s1 = c2f[:, 0:DM]
    x2 = c2f[:, DM:2 * DM]
    ln3 = C2[:, 4 * DM:5 * DM]
    ln3t = C2[:, 5 * DM:5 * DM + NCH * 128]

    psA = nc.alloc_psum_tensor("psA", [128, 1024], F32)
    psB = nc.alloc_psum_tensor("psB", [128, 1024], F32)
    psC = nc.alloc_psum_tensor("psC", [128, 1024], F32)
    psD = nc.alloc_psum_tensor("psD", [128, 512], F32)
    psE = nc.alloc_psum_tensor("psE", [128, 512], F32)
    psA_bf = psA.bitcast(BF16)
    psB_bf = psB.bitcast(BF16)
    psC_bf = psC.bitcast(BF16)
    psD_bf = psD.bitcast(BF16)
    psE_bf = psE.bitcast(BF16)
    schalf = [psA[:, 0:GQ], psA[:, 512:512 + GQ],
              psB[:, 0:GQ], psB[:, 512:512 + GQ]]

    for s in ("CONST", "CONSTG", "ZW0", "ZW1", "OUT0", "OUT1", "WO", "PE",
              "DV", "SC", "GP"):
        p.sem(s)
    XAs = [p.sem(f"XA{i}") for i in range(4)]
    XCs = [p.sem(f"XC{i}") for i in range(2)]
    ZRs = [p.sem(f"ZR{i}") for i in range(2)]
    FWs = [p.sem(f"FW{i}") for i in range(2)]
    WSs = [p.sem(f"WS{i}") for i in range(4)]
    ZWs = ("ZW0", "ZW1")
    OUTs = ("OUT0", "OUT1")

    for dst, src in ((eye_sb, eye_h), (tri_sb, tri_h), (onec_sb, onec_h),
                     (oner_sb, oner_h), (onem_sb, onem_h), (bqkvT_sb, bqkvT_h),
                     (epsb_sb, epsb_h), (lncb_sb, lncb_h)):
        p.op("sync", lambda e, d=dst, s=src: e.dma_start(d[:], s.ap()),
             inc="CONST", incn=16)
    const_t = p.cnt("CONST")
    p.op("gpsimd", lambda e: e.dma_start(
        bo_sb[:], bo_h.ap().rearrange("(o d) -> o d", o=1)), inc="CONSTG", incn=16)
    p.op("gpsimd", lambda e: e.dma_start(
        fb_sb[:], fb_h.ap().rearrange("(o d) -> o d", o=1)), inc="CONSTG", incn=16)
    constg_t = p.cnt("CONSTG")

    w_handles = (wq_h, wk_h, wv_h)
    ws_t = {}
    pe_w_done = {}

    def emit_w_load(h, wi):
        slot = (3 * h + wi) % 4
        wait = []
        m = 3 * h + wi - 4
        if m >= 0:
            hu, iu = divmod(m, 3)
            wait.append(("PE", pe_w_done[(hu, iu)]))
        src = w_handles[wi].ap()[h].rearrange("(c q) e -> q c e", q=128)
        ws_t[(h, wi)] = p.op(
            "gpsimd", lambda e, d=wslot[slot], s=src: e.dma_start(d, s),
            wait=wait, inc=WSs[slot], incn=16)

    for wi in range(3):
        emit_w_load(0, wi)

    def emit_ln_stats(src_ap, par, wait, war_sc=0):
        for g in range(NST):
            w = list(wait) if g == 0 else []
            if g == 0 and war_sc:
                w.append(("SC", war_sc))
            p.op("vector", lambda e, g=g, a=src_ap, pr=par: e.bn_stats(
                st6[pr][:, 6 * g:6 * g + 6], a[:, STW * g:STW * (g + 1)]), wait=w)
        p.drain("vector")
        return p.op("vector", lambda e, pr=par: e.bn_aggr(mv2[pr][:], st6[pr][:]),
                    inc="DV")

    def emit_rstd(par, wait_dv, war_dv=0, extra=()):
        w = [("DV", wait_dv)] + [t for t in extra if t[1]]
        if war_dv:
            w.append(("DV", war_dv))
        w.append(("CONST", const_t))
        p.op("scalar", lambda e, pr=par: e.activation(
            rstd[pr][:], mv2[pr][:, 1:2], AF.Ln, bias=epsb_sb[:, 0:1],
            scale=1.0), wait=w)
        p.drain("scalar")
        return p.op("scalar", lambda e, pr=par: e.activation(
            rstd[pr][:], rstd[pr][:], AF.Exp, bias=0.0, scale=-0.5), inc="SC")

    dv_apply = {}
    dv_evacA = {}
    pe_trA = {}
    sc_rstdA = {}
    ln1bf = [v_sb, ptil[:, SEQ:2 * SEQ]]
    xa_tot = [0, 0, 0, 0]

    def emit_A_head(t):
        slot = t % 4
        par = t % 3
        wait = [("DV", dv_apply[t - 4])] if t - 4 in dv_apply else []
        xt = p.op(("sync", "gpsimd")[slot % 2], lambda e, sl=slot, t=t:
                  e.dma_start(xa[sl], x_h.ap()[t * 128:(t + 1) * 128, :]),
                  wait=wait, inc=XAs[slot], incn=16)
        xa_tot[slot] = xt
        aggr = emit_ln_stats(xa[slot], par, [(XAs[slot], xt)],
                             war_sc=sc_rstdA.get(t - 3, 0))
        sc_rstdA[t] = emit_rstd(par, aggr, war_dv=dv_apply.get(t - 3, 0))

    def emit_A_tail(t):
        par = t % 3
        slot = t % 4
        bpar = t % 2
        wait = [("SC", sc_rstdA[t])]
        if t - 2 in pe_trA:
            wait.append(("PE", pe_trA[t - 2]))
        p.drain("vector")
        hn2 = max(NCH // 2, 1)
        hw = hn2 * 128
        dv_apply_h = [0, 0]
        dv_apply_h[0] = p.op("vector", lambda e, pr=par, sl=slot, bp=bpar,
                             hw=hw: e.tensor_scalar(
            ln1bf[bp][:, 0:hw], xa[sl][:, 0:hw], mv2[pr][:, 0:1],
            rstd[pr][:, 0:1], ALU.subtract, ALU.mult), wait=wait, inc="DV")
        dv_apply_h[1] = dv_apply[t] = p.op(
            "vector", lambda e, pr=par, sl=slot, bp=bpar, hw=hw:
            e.tensor_scalar(
                ln1bf[bp][:, hw:NCH * 128], xa[sl][:, hw:NCH * 128],
                mv2[pr][:, 0:1], rstd[pr][:, 0:1],
                ALU.subtract, ALU.mult), inc="DV") \
            if NCH > 1 else dv_apply_h[0]
        if NCH == 1:
            dv_apply[t] = dv_apply_h[0]
        last_sc = 0
        for half in range(2 if NCH > 1 else 1):
            b = (psD_bf, psE_bf)[half]
            c0 = half * hn2
            c1 = NCH if half else hn2
            wait = [("DV", dv_apply_h[half]), ("CONST", const_t),
                    ("SC", sc_evacA.get((t - 1, half), 0))]
            for c in range(c0, c1):
                w = wait if c == c0 else ()
                tgt = p.op("tensor", lambda e, c=c, b=b, c0=c0, bp=bpar:
                           e.transpose(b[:, (c - c0) * 128:(c - c0 + 1) * 128],
                                       ln1bf[bp][:, c * 128:(c + 1) * 128],
                                       eye_sb[:]), wait=w,
                           inc="PE" if c == c1 - 1 else None)
            pe_trA[t] = tgt
            p.drain("scalar")
            last_sc = p.op("scalar", lambda e, t=t, b=b, c0=c0, c1=c1: e.copy(
                ln1t3[:, c0:c1, t * 128:(t + 1) * 128],
                b[:, 0:(c1 - c0) * 128].rearrange("p (c q) -> p c q", c=c1 - c0)),
                wait=[("PE", pe_trA[t])], inc="SC")
            sc_evacA[(t, half)] = last_sc
        sc_evacA[t] = last_sc
        dv_evacA[t] = 0

    qkv_dst = (qt_sb[:], kt_sb[:], vstage)
    psbuf_last_rd = [0, 0, 0, 0]
    ptil_slot_av = [0] * NT
    pe_av_all = {}
    dv_zevac = {}
    zw_par_t = [0, 0]
    dv_bank_free = 0
    pe_norm_last = 0
    dv_vch = {}
    dv_vch_lo = {}
    gidx = 0
    sc_rinv_prev = 0
    dv_mbias_prev = 0
    pe_mb_prev = 0
    dv_red_last = 0
    dv_mx_d = {1: 0}
    dv_sq_d = {}
    dv_red_d = {}
    dv_nm_cur = 0
    dv_mbias = 0
    sc_lns = 0
    rbc_war = [0, 0]
    sc_after_masks = 0

    SET0 = None
    dv_qkv_evac_d = {}
    dv_qkv_evac_s = {}
    pe_w_done_d = {}

    sc_bw_evac_half = [0, 0]
    sc_sign_d = {}
    dv_fw_free = {}
    pe_bw_last = 0
    pe_bw_par = [0, 0]
    pe_bw_tr = {}

    def emit_w_mms(h, wi, only_slice=None, defer_evac=False):
        nonlocal dv_bank_free
        bankset = [[psD[:, 0:SLW], psE[:, 0:SLW], psC[:, 0:SLW],
                    psC[:, 512:512 + SLW]],
                   [psA[:, 0:SLW], psA[:, 512:512 + SLW], psB[:, 0:SLW],
                    psB[:, 512:512 + SLW]]]
        sidx = 0 if wi == 1 else 1
        banks = bankset[sidx]
        slot = (3 * h + wi) % 4
        dst = (qt_sb[:], kt_sb[:], vstage)[wi]
        tpsl = NT // NSL
        pe_mm = 0
        deferred = []
        if only_slice is not None:
            slices = [(only_slice, 1)]
        else:
            slices = [(sq0, min(4, NSL - sq0)) for sq0 in range(0, NSL, 4)]
        for sq0, nq in slices:
            for c in range(NCH):
                wait = []
                if c == 0:
                    wait = [(WSs[slot], ws_t[(h, wi)])]
                    if only_slice is not None:
                        wait.append(("DV", dv_qkv_evac_d.get((h, wi), 0)))
                    if sidx == 0:
                        wait += [("DV", dv_zevac.get(gidx - 1, 0)),
                                 ("DV", dv_mbias_prev),
                                 ("SC", sc_rinv_prev),
                                 ("SC", max(sc_bw_evac_half)),
                                 ("DV", dv_qkv_evac_d.get((h, 0), dv_bank_free))]
                    else:
                        wait += [("SC", max(psbuf_last_rd)),
                                 ("DV", dv_vch.get(h - 1, 0)),
                                 ("DV", dv_red_last)]
                        if wi == 2:
                            wait.append(("DV", dv_qkv_evac_d.get((h, 0), 0)))
                for k in range(nq):
                    s = sq0 + k
                    w = list(wait) if k == 0 else []
                    if c == 0:
                        w.append(("SC", sc_evacA[min((s + 1) * tpsl, NT) - 1]))
                    w = w or ()
                    bidx = (2 + s % 2) if (only_slice is not None and wi == 1) \
                        else (s % 4)
                    tgt = p.op(
                        "tensor", lambda e, c=c, s=s, sl=slot, banks=banks,
                        bidx=bidx:
                        e.matmul(banks[bidx],
                                 wslot[sl][:, c * 128:(c + 1) * 128],
                                 ln1t3[:, c, s * SLW:(s + 1) * SLW],
                                 start=(c == 0), stop=(c == NCH - 1)),
                        wait=w,
                        inc="PE" if (c == NCH - 1 and k == nq - 1) else None)
            pe_mm = tgt

            def do_evacs(sq0=sq0, nq=nq, pe_mm=pe_mm, banks=banks):
                nonlocal dv_bank_free
                for k in range(nq):
                    s = sq0 + k
                    bidx = (2 + s % 2) if (only_slice is not None and
                                           wi == 1) else (s % 4)
                    dv_bank_free = p.op(
                        "vector", lambda e, s=s, d=dst, wi=wi, h=h,
                        banks=banks, bidx=bidx:
                        e.tensor_scalar(d[:, s * SLW:(s + 1) * SLW],
                                        banks[bidx],
                                        bqkvT_sb[:, wi * H + h:
                                                 wi * H + h + 1],
                                        None, ALU.add),
                        wait=[("PE", pe_mm)], inc="DV")
                    dv_qkv_evac_s[(h, wi, s)] = dv_bank_free
            if defer_evac:
                deferred.append(do_evacs)
            else:
                do_evacs()
        if (h, wi) in pe_w_done_d:
            pe_mm = max(pe_mm, pe_w_done_d[(h, wi)])
        pe_w_done_d[(h, wi)] = pe_mm
        pe_w_done[(h, wi)] = pe_mm

        def finish():
            nonlocal dv_bank_free
            for f in deferred:
                f()
            cur = dv_qkv_evac_d.get((h, wi), 0)
            dv_qkv_evac_d[(h, wi)] = max(cur, dv_bank_free)
        if defer_evac:
            return finish
        finish()

    sc_evacA = {}
    wA_flush = {}
    for t in range(NT + 2):
        if t < NT:
            emit_A_head(t)
        if t - 2 >= 0:
            emit_A_tail(t - 2)
            tp = t - 2
            for f in wA_flush.pop(tp, ()):
                f()
            if (tp + 1) % (NT // NSL) == 0 and H > 0:
                s = (tp + 1) // (NT // NSL) - 1
                flq = emit_w_mms(0, 0, only_slice=s, defer_evac=True)
                flk = emit_w_mms(0, 1, only_slice=s, defer_evac=True)
                flq()
                flv = emit_w_mms(0, 2, only_slice=s, defer_evac=True)
                wA_flush[tp + 3] = [flk, flv]
    for fl in sorted(wA_flush):
        for f in wA_flush[fl]:
            f()

    sc_ln1t_all = sc_evacA[NT - 1]

    def emit_bw_evacs(t):
        hn2 = max(NCH // 2, 1)
        for half in range(2 if NCH > 1 else 1):
            b = (psD_bf, psE_bf)[half]
            c0 = half * hn2
            c1 = NCH if half else hn2
            sc_bw_evac_half[half] = p.op(
                "scalar", lambda e, t=t, b=b, c0=c0, c1=c1: e.copy(
                    bwt3[:, c0:c1, t * 128:(t + 1) * 128],
                    b[:, 0:(c1 - c0) * 128].rearrange("p (c q) -> p c q",
                                                      c=c1 - c0)),
                wait=[("PE", pe_bw_tr[(t, half)])], inc="SC")

    def emit_bw_tile(t):
        nonlocal pe_bw_last
        fpar = t % 2
        stg = (v_sb, kt_sb)[fpar]
        wait = [("SC", sc_ln1t_all),
                (XAs[1], xa_tot[1]), (XAs[2], xa_tot[2])] if t < 2 else []
        if t - 2 in dv_fw_free:
            wait = [("DV", dv_fw_free[t - 2]), ("SC", sc_sign_d[t - 2])]
        fw_t = p.op("gpsimd", lambda e, t=t, fpar=fpar: e.dma_start(
            fwst[fpar], fw_h.ap()[t * 128:(t + 1) * 128, :]),
            wait=wait, inc=FWs[fpar], incn=16)
        dv_as = p.op("vector", lambda e, fpar=fpar: e.reduce_sum(
            asum[:], fwst[fpar], AX.X, apply_absolute_value=True),
            wait=[(FWs[fpar], fw_t)], inc="DV")
        p.drain("vector")
        dv_al = p.op("vector", lambda e: e.tensor_scalar(
            twoal[:], asum[:], 1.0 / DM, None, ALU.mult), inc="DV")
        dv_fw_free[t] = dv_as
        w2 = [(FWs[fpar], fw_t), ("PE", pe_bw_par[fpar])]
        if t < 2:
            w2.append(("PE", pe_av_all[H - 1]))
        sc_sign_d[t] = p.op("scalar", lambda e, fpar=fpar, stg=stg:
                            e.activation(stg[:], fwst[fpar], AF.Sign,
                                         bias=0.0, scale=1.0),
                            wait=w2, inc="SC",
                            lowp="binarized weight is a bf16 matmul operand")
        if t >= 1:
            emit_bw_evacs(t - 1)
        dv_bw = p.op("vector", lambda e, stg=stg: e.tensor_scalar(
            stg[:], stg[:], twoal[:, 0:1], None, ALU.mult),
            wait=[("SC", sc_sign_d[t]), ("DV", dv_al)], inc="DV",
            lowp="binarized weight is a bf16 matmul operand")
        hn2 = max(NCH // 2, 1)
        for half in range(2 if NCH > 1 else 1):
            b = (psD_bf, psE_bf)[half]
            c0 = half * hn2
            c1 = NCH if half else hn2
            wait = [("DV", dv_bw), ("SC", sc_bw_evac_half[half]),
                    ("DV", dv_zevac.get(gidx - 1, 0))]
            for c in range(c0, c1):
                w3 = wait if c == c0 else ()
                tgt = p.op("tensor", lambda e, c=c, b=b, c0=c0, stg=stg:
                           e.transpose(b[:, (c - c0) * 128:(c - c0 + 1) * 128],
                                       stg[:, c * 128:(c + 1) * 128],
                                       eye_sb[:]), wait=w3,
                           inc="PE" if c == c1 - 1 else None)
            pe_bw_tr[(t, half)] = tgt
            pe_bw_last = tgt
        pe_bw_par[fpar] = pe_bw_last

    def emit_mh_sq(which, h):
        src = (qt_sb, kt_sb)[which]
        if which == 0:
            w0 = [("DV", dv_qkv_evac_d[(h, 0)]), ("PE", pe_mb_prev),
                  ("PE", pe_bw_last), ("PE", pe_norm_last)]
        else:
            w0 = [("DV", dv_qkv_evac_d[(h, 1)]), ("PE", pe_norm_last)]
        dv_sq_d[which] = p.op("vector", lambda e, s=src: e.tensor_mul(
            sqb, s[:], s[:]), wait=w0, inc="DV")

    def emit_mh_mms(which, h):
        nonlocal pe_norm_last, dv_red_last, sc_lns
        for s in range(NSL):
            hb = s % 4
            wait = [("DV", dv_sq_d[which]), ("CONST", const_t),
                    ("SC", psbuf_last_rd[hb])]
            if which == 1:
                wait.append(("DV", dv_red_d[(0, s)]))
                wait.append(("DV", dv_qkv_evac_s.get(
                    (h, 2, s), dv_qkv_evac_d[(h, 2)])))
            elif s == 0 and h == 0:
                wait.append(("DV", dv_qkv_evac_d[(h, 2)]))
            pe_norm_last = p.op("tensor", lambda e, s=s, hb=hb: e.matmul(
                [psA[:, 0:SLW], psA[:, 512:512 + SLW], psB[:, 0:SLW],
                 psB[:, 512:512 + SLW]][hb], onem_sb[:],
                sqb[:, s * SLW:(s + 1) * SLW], start=True, stop=True),
                wait=wait, inc="PE")
            dv_red_d[(which, s)] = p.op(
                "vector", lambda e, s=s, hb=hb, wh=which: e.reduce_max(
                    srow[0:1, wh * 8 + s:wh * 8 + s + 1],
                    [psA[0:1, 0:SLW], psA[0:1, 512:512 + SLW],
                     psB[0:1, 0:SLW], psB[0:1, 512:512 + SLW]][hb], AX.X),
                wait=[("PE", pe_norm_last)], inc="DV")
        p.drain("vector")
        dv_mx_d[which] = p.op("vector", lambda e, wh=which: e.reduce_max(
            srow[0:1, 16 + wh:17 + wh], srow[0:1, wh * 8:wh * 8 + NSL],
            AX.X), inc="DV")
        dv_red_last = dv_mx_d[which]
        p.drain("scalar")
        t_ln = p.op("scalar", lambda e, wh=which: e.activation(
            srow[0:1, 18 + wh:19 + wh], srow[0:1, 16 + wh:17 + wh], AF.Ln,
            bias=0.0, scale=1.0), wait=[("DV", dv_mx_d[which])],
            inc="SC" if which == 1 else None)
        if which == 1:
            sc_lns = t_ln

    def emit_mh_tail(h):
        nonlocal dv_nm_cur
        p.drain("vector")
        dv_add = p.op("vector", lambda e: e.tensor_tensor(
            srow[0:1, 20:21], srow[0:1, 18:19], srow[0:1, 19:20], ALU.add),
            wait=[("SC", sc_lns)], inc="DV")
        sc_M = p.op("scalar", lambda e: e.activation(
            srow[0:1, 21:22], srow[0:1, 20:21], AF.Exp,
            bias=lncb_sb[0:1, 0:1], scale=0.5),
            wait=[("DV", dv_add)], inc="SC")
        p.drain("vector")
        dv_nm_cur = p.op("vector", lambda e: e.tensor_scalar(
            nm_bf[:], srow[0:1, 21:22], -1.0, None, ALU.mult),
            wait=[("SC", sc_M)], inc="DV", lowp="softmax shift operand; "
            "softmax is shift-invariant so the value need not be exact")

    def emit_mb(h):
        nonlocal pe_mb_prev, dv_mbias, dv_mbias_prev
        pe_mb = p.op("tensor", lambda e: e.matmul(
            psC[:, 0:1], oner_sb[:], nm_bf[:], start=True, stop=True),
            wait=[("DV", dv_nm_cur), ("DV", dv_qkv_evac_d[(h, 1)]),
                  ("SC", sc_rinv_prev)], inc="PE")
        pe_mb_prev = pe_mb
        dv_mbias = p.op("vector", lambda e: e.tensor_copy(
            mbias[:], psC[:, 0:1]), wait=[("PE", pe_mb)], inc="DV")
        dv_mbias_prev = dv_mbias

    for h in range(H):
        if h > 0:
            emit_w_mms(h, 1)
            emit_mh_sq(1, h)
            emit_w_mms(h, 2)
        dv_qkv_evac = [dv_qkv_evac_d[(h, wi)] for wi in range(3)]
        dv_qkv_all = max(dv_qkv_evac)
        if h + 1 < H:
            for wi in range(3):
                emit_w_load(h + 1, wi)
        wo_start = max(0, H - 4)
        if h >= wo_start:
            win = H - wo_start
            per = (H + win - 1) // win
            hh0 = (h - wo_start) * per
            for hh in range(hh0, min(hh0 + per, H)):
                p.op("gpsimd", lambda e, hh=hh: e.dma_start(
                    Bz[:, hh * DM:(hh + 1) * DM], wo_h.ap()[hh]),
                    inc="WO", incn=16)

        if h == 0:
            emit_mh_sq(0, 0)
            emit_mh_mms(0, 0)
            emit_mh_sq(1, 0)
        emit_mh_mms(1, h)
        emit_mh_tail(h)

        wait = [("DV", dv_qkv_all), ("DV", dv_mx_d[1])]
        nh = NT // 2
        pe_vtr_half = 0
        for t in range(NT):
            w = wait if t == 0 else ()
            tgt = p.op("tensor", lambda e, t=t: e.transpose(
                psA_bf[:, t * 128:(t + 1) * 128], vstage[:, t * 128:(t + 1) * 128],
                eye_sb[:]), wait=w,
                inc="PE" if t in (nh - 1, NT - 1) else None)
            if t == nh - 1:
                pe_vtr_half = tgt
        pe_vtr = tgt
        for j in range(max(SEQ // GQ, 1)):
            ptil_slot_av[j] = max(ptil_slot_av[j], pe_vtr)
        dv_vch_lo[h] = p.op("vector", lambda e, nh=nh: e.tensor_copy(
            v_sb[:, 0:nh * 128], psA_bf[:, 0:nh * 128]),
            wait=[("PE", pe_vtr_half)], inc="DV")
        dv_vch[h] = p.op("vector", lambda e, nh=nh: e.tensor_copy(
            v_sb[:, nh * 128:NT * 128], psA_bf[:, nh * 128:NT * 128]),
            wait=[("PE", pe_vtr)], inc="DV")

        estep = 2 if GQ == 512 else 1

        def emit_score(g, j, pe_sc):
            hb = j % 4
            wait = [("SC", psbuf_last_rd[hb])]
            split_ok = NT * 128 == 2048
            if j == 0:
                wait.append(("DV", dv_qkv_all))
                wait.append(("DV", dv_vch_lo[h] if (g == 0 and split_ok)
                             else dv_vch[h]))
                if g == 0:
                    wait.append(("DV", dv_mx_d[1]))
            elif g == 0 and j == 1 and split_ok:
                wait.append(("DV", dv_vch[h]))
            pe_sc[j] = p.op("tensor", lambda e, j=j, g=g, hb=hb: e.matmul(
                schalf[hb], kt_sb[:, j * 128:(j + 1) * 128],
                qt_sb[:, g * GQ:(g + 1) * GQ], start=True, stop=True),
                wait=wait, inc="PE")

        def emit_exp(g, j0, pe_sc, sc_exp):
            if estep == 2:
                buf = psA[:] if (j0 % 4 == 0) else psB[:]
                dst = ptil[:, j0 * GQ:(j0 + 2) * GQ]
            else:
                buf = schalf[j0 % 4]
                dst = pslot[j0]
            wait = [("PE", pe_sc[j0 + estep - 1]), ("DV", dv_mbias),
                    ("PE", max(ptil_slot_av[j0:j0 + estep]))]
            t_exp = p.op("scalar", lambda e, buf=buf, dst=dst: e.activation(
                dst, buf, AF.Exp, bias=mbias[:, 0:1], scale=C_ATTN),
                wait=wait, inc="SC")
            for jj in range(j0, j0 + estep):
                sc_exp[jj] = t_exp
                psbuf_last_rd[jj % 4] = t_exp

        pe_sc = {}
        sc_exp = {}
        mb_done = False
        for j in range(TPG):
            emit_score(0, j, pe_sc)
            if (j + 1) % estep == 0:
                if not mb_done:
                    emit_mb(h)
                    mb_done = True
                emit_exp(0, j + 1 - estep, pe_sc, sc_exp)
        for g in range(NG):
            chunks = (g + 1) * TPG
            gp_mask_j = {}
            for r in range(TPG):
                j = g * TPG + r
                if r > 0:
                    p.op("gpsimd", lambda e, j=j, r=r: e.memset(
                        ptil[:, j * GQ:j * GQ + r * 128], 0.0),
                        wait=[("SC", sc_exp[j])], inc="GP")
                gp_mask_j[j] = p.op("gpsimd", lambda e, j=j, r=r: e.tensor_tensor(
                    ptil[:, j * GQ + r * 128:j * GQ + (r + 1) * 128],
                    ptil[:, j * GQ + r * 128:j * GQ + (r + 1) * 128],
                    tri_sb[:], ALU.mult), wait=[("SC", sc_exp[j])], inc="GP")
            pe_den = 0
            for j in range(chunks):
                wait = [("SC", sc_exp[j])] if j < g * TPG else \
                    [("GP", gp_mask_j[j])]
                if j == 0:
                    wait.append(("SC", sc_rinv_prev))
                pe_den = p.op("tensor", lambda e, j=j, chunks=chunks: e.matmul(
                    psC[:, 512:512 + GQ], onem_sb[:], pslot[j],
                    start=(j == 0), stop=(j == chunks - 1)),
                    wait=wait, inc="PE" if j == chunks - 1 else None)
            p.drain("scalar")
            p.op("scalar", lambda e: e.activation(
                psC[:, 512:512 + GQ], psC[:, 512:512 + GQ], AF.Ln,
                bias=0.0, scale=1.0), wait=[("PE", pe_den)])
            p.drain("scalar")
            sc_rinv = p.op("scalar", lambda e, g=g: e.activation(
                rbc[g % 2], psC[:, 512:512 + GQ], AF.Exp, bias=0.0, scale=-1.0),
                wait=[("DV", rbc_war[g % 2])], inc="SC",
                lowp="softmax normalizer becomes a bf16 matmul operand")
            sc_rinv_prev = sc_rinv
            dv_norm = {}
            for j in range(chunks):
                wait = [("SC", sc_rinv)] if j == 0 else []
                if j == 0:
                    wait.append(("PE", pe_den))
                dv_norm[j] = p.op("vector", lambda e, j=j, g=g: e.tensor_tensor(
                    pslot[j], pslot[j], rbc[g % 2], ALU.mult),
                    wait=wait, inc="DV")
            rbc_war[g % 2] = dv_norm[chunks - 1]
            nchunks = (g + 2) * TPG
            if g + 1 < NG:
                pe_sc = {}
                for j in range(min(4, nchunks)):
                    emit_score(g + 1, j, pe_sc)
            elif h + 1 < H:
                emit_w_mms(h + 1, 0)
                emit_mh_sq(0, h + 1)
            pe_av = 0
            for j in range(chunks):
                wait = [("DV", dv_norm[j])]
                if j == 0:
                    wait += [("DV", dv_zevac.get(gidx - 1, 0)),
                             ("SC", max(sc_bw_evac_half))]
                pe_av = p.op("tensor", lambda e, j=j, chunks=chunks: e.matmul(
                    psE[:, 0:GQ], v_sb[:, j * 128:(j + 1) * 128], pslot[j],
                    start=(j == 0), stop=(j == chunks - 1)),
                    wait=wait, inc="PE")
                ptil_slot_av[j] = pe_av
            pe_av_all[h] = pe_av
            if g + 1 < NG:
                sc_exp = {}
                for j0 in range(0, min(4, nchunks), estep):
                    emit_exp(g + 1, j0, pe_sc, sc_exp)
                for j in range(min(4, nchunks), nchunks):
                    emit_score(g + 1, j, pe_sc)
                    if (j + 1) % estep == 0 and j + 1 > 4:
                        emit_exp(g + 1, j + 1 - estep, pe_sc, sc_exp)
            zpar = gidx % 2
            wait = [("PE", pe_av), (ZWs[zpar], zw_par_t[zpar])]
            dv_zevac[gidx] = p.op("vector", lambda e, zpar=zpar: e.tensor_copy(
                zstage[zpar], psE[:, 0:GQ]), wait=wait, inc="DV")
            zw_par_t[zpar] = p.op(
                "sync", lambda e, g=g, h=h, zpar=zpar: e.dma_start(
                    zs_h.ap()[g * TPG:(g + 1) * TPG, h].transpose([1, 0, 2]),
                    zstage[zpar].rearrange("p (t q) -> p t q", q=128)),
                wait=[("DV", dv_zevac[gidx])], inc=ZWs[zpar], incn=16)
            gidx += 1
            if g == NG - 1 and h + 1 < H:
                emit_mh_mms(0, h + 1)

    B_PE = p.cnt("PE")
    B_DV = p.cnt("DV")
    B_SC = p.cnt("SC")
    B_GP = p.cnt("GP")
    zw_tot = [p.cnt("ZW0"), p.cnt("ZW1")]
    wo_tot = p.cnt("WO")
    fw_tot = [p.cnt("FW0"), p.cnt("FW1")]
    ws_tot = [p.cnt(f"WS{i}") for i in range(4)]

    pe_proj = {}
    pe_ffn = {}
    sc_s1 = {}
    sc_s2 = {}
    gp_x2 = {}
    gp_x3 = {}
    dv_aggr2 = {}
    dv_ct = {}
    dv_ln3 = {}
    dv_ln3t = {}
    dv_x3ts = {}
    sc_r2 = {}
    sc_r3_prev = 0
    sc_r2_prev = 0
    sc_r4_prev = 0
    dv_ln3_prev = 0
    out_prev = 0
    xc_t = {}
    pe_tr3_d = {}
    ln3v = [ln3, v_sb[:, 0:DM]]
    projb = [psA[:, 0:PSW], psA[:, 512:512 + PSW],
             psB[:, 0:PSW], psB[:, 512:512 + PSW]][:NPS]
    ffnb = [psC[:, 0:PSW], psC[:, 512:512 + PSW],
            psD[:, 0:PSW], psE[:, 0:PSW]][:NPS]
    hn = max(NCH // 2, 1)

    def emit_front_pe(i):
        nonlocal sc_r2_prev
        par = i % 2
        wait = [("PE", pe_proj[i - 2])] if i - 2 in pe_proj else \
            [("PE", B0_PE), ("DV", B0_DV)]
        if i < 2:
            wait += [("ZW0", zw_tot[0]), ("ZW1", zw_tot[1])]
        if i == 1:
            wait.append(("PE", B_PE))
        zrd = (qt_sb, kt_sb)[par]
        zr_t = p.op("sync", lambda e, i=i, d=zrd: e.dma_start(
            d[:, 0:H * 128].rearrange("p (h q) -> p h q", q=128),
            zs_h.ap()[i].transpose([1, 0, 2])), wait=wait, inc=ZRs[par], incn=16)
        wait = [("DV", gp_x3[i - 2])] if i - 2 in gp_x3 else \
            [("PE", B0_PE), ("DV", B0_DV), ("SC", B0_SC), ("GP", B0_GP),
             (XAs[3], xa_tot[3])]
        xc_t[i] = p.op("sync", lambda e, i=i, par=par: e.dma_start(
            xin[par], x_h.ap()[i * 128:(i + 1) * 128, :]),
            wait=wait, inc=XCs[par], incn=16)
        for hh in range(H):
            for s in range(NPS):
                wait = []
                if hh == 0 and s == 0:
                    wait = [(ZRs[par], zr_t), ("WO", wo_tot),
                            ("SC", sc_s1.get(i - 1, B0_SC if i == 0 else 0)),
                            ("CONSTG", constg_t)]
                    if i == 0:
                        wait.append(("DV", B0_DV))
                p.op("tensor", lambda e, hh=hh, s=s, d=zrd: e.matmul(
                    projb[s], d[:, hh * 128:(hh + 1) * 128],
                    Bz[:, hh * DM + s * 512:hh * DM + s * 512 + PSW],
                    start=(hh == 0), stop=False), wait=wait)
        for s in range(NPS):
            tgt = p.op("tensor", lambda e, s=s: e.matmul(
                projb[s], oner_sb[:], bo_sb[0:1, s * 512:s * 512 + PSW],
                start=False, stop=True), inc="PE" if s == NPS - 1 else None)
        pe_proj[i] = tgt

    def emit_front_ln2(i):
        nonlocal sc_r2_prev
        par = i % 2
        wait = [("PE", pe_proj[i]), ("DV", dv_aggr2.get(i - 1, 0)),
                ("DV", gp_x2.get(i - 1, 0))]
        if i == 0:
            wait += [("DV", B0_DV), ("GP", B0_GP), ("ZW0", zw_tot[0]),
                     ("ZW1", zw_tot[1])]
        w1 = min(1024, DM)
        if NPS > 2:
            sc_a = p.op("scalar", lambda e, w1=w1: e.copy(
                s1[:, 0:w1], psA[:, 0:w1]), wait=wait, inc="SC")
            p.drain("scalar")
            sc_s1[i] = p.op("scalar", lambda e, w1=w1: e.copy(
                s1[:, w1:2 * w1], psB[:, 0:w1]), inc="SC")
        else:
            sc_a = sc_s1[i] = p.op("scalar", lambda e, w1=w1: e.copy(
                s1[:, 0:w1], psA[:, 0:w1]), wait=wait, inc="SC")
        for g in range(NST):
            hw = sc_a if (g * STW < w1) else sc_s1[i]
            w = [("SC", hw)]
            if g == 0:
                w.append(("SC", sc_r4_prev))
                if i == 0:
                    w.append(("GP", B_GP))
            p.op("vector", lambda e, g=g: e.bn_stats(
                st6[0][:, 6 * g:6 * g + 6], s1[:, STW * g:STW * (g + 1)]), wait=w)
        p.drain("vector")
        dv_aggr2[i] = p.op("vector", lambda e: e.bn_aggr(mv2[0][:], st6[0][:]),
                           inc="DV")
        sc_r2[i] = emit_rstd(0, dv_aggr2[i],
                             war_dv=max(dv_ct.get(i - 1, 0),
                                        gp_x2.get(i - 1, 0)))
        sc_r2_prev = sc_r2[i]
        p.drain("vector")
        dv_ct[i] = p.op("vector", lambda e, par=par: e.tensor_tensor(
            corr[:, par:par + 1], mv2[0][:, 0:1], rstd[0][:, 0:1], ALU.mult),
            wait=[("SC", sc_r2[i])], inc="DV")

    def emit_front_chain(i):
        nonlocal sc_r3_prev, dv_ln3_prev
        par = i % 2
        wait = [("SC", sc_r2[i]), (XCs[par], xc_t[i])]
        if i == 0:
            wait += [(FWs[0], fw_tot2[0]), (FWs[1], fw_tot2[1]),
                     ("DV", B_DV)]
        p.drain("vector")
        gp_x2[i] = p.op("vector", lambda e, par=par: e.scalar_tensor_tensor(
            xin[par], s1, rstd[0][:, 0:1], xin[par], ALU.mult, ALU.add),
            wait=wait, inc="DV")
        aggr3 = emit_ln_stats(xin[par], 1, [("DV", gp_x2[i])],
                              war_sc=sc_r3_prev)
        sc_r3 = emit_rstd(1, aggr3, war_dv=dv_ln3_prev)
        sc_r3_prev = sc_r3
        p.drain("vector")
        w3 = [("SC", sc_r3), ("PE", pe_tr3_d.get(i - 2, 0))]
        if i < 2:
            w3 += [("PE", B_PE), ("SC", B_SC)]
        dv_ln3[i] = p.op("vector", lambda e, par=par: e.tensor_scalar(
            ln3v[par], xin[par], mv2[1][:, 0:1], rstd[1][:, 0:1],
            ALU.subtract, ALU.mult), wait=w3, inc="DV")
        dv_ln3_prev = dv_ln3[i]

    def emit_back_tr_ffn(i):
        par = i % 2
        wait = [("DV", dv_ln3[i]),
                ("SC", sc_s2.get(i - 1, B_SC if i == 0 else 0))]
        if i == 0:
            wait.append(("DV", B_DV))
        pe_half = 0
        for c in range(NCH):
            w = wait if c == 0 else ()
            buf = psD_bf if c < hn else psE_bf
            cc = c % hn
            tgt = p.op("tensor", lambda e, c=c, buf=buf, cc=cc, par=par:
                       e.transpose(buf[:, cc * 128:(cc + 1) * 128],
                                   ln3v[par][:, c * 128:(c + 1) * 128],
                                   eye_sb[:]), wait=w,
                       inc="PE" if c in (hn - 1, NCH - 1) else None)
            if c == hn - 1:
                pe_half = tgt
        pe_tr3 = tgt
        pe_tr3_d[i] = pe_tr3
        dv_lo = p.op("vector", lambda e: e.tensor_copy(
            ln3t[:, 0:hn * 128], psD_bf[:, 0:hn * 128]),
            wait=[("PE", pe_half)], inc="DV")
        dv_ln3t[i] = p.op("vector", lambda e: e.tensor_copy(
            ln3t[:, hn * 128:NCH * 128], psE_bf[:, 0:(NCH - hn) * 128]),
            wait=[("PE", pe_tr3)], inc="DV") if NCH > hn else dv_lo
        for c in range(NCH):
            for s in range(NPS):
                wait = []
                if c == 0:
                    wait = [("DV", dv_ln3t[i] if s == NPS - 1 else dv_lo)]
                    if s == 0:
                        wait.append(
                            ("SC", sc_s2.get(i - 1, B_SC if i == 0 else 0)))
                        if i == 0:
                            wait.append(("DV", B_DV))
                p.op("tensor", lambda e, c=c, s=s: e.matmul(
                    ffnb[s], ln3t[:, c * 128:(c + 1) * 128],
                    bwt3[:, c, s * 512:s * 512 + PSW],
                    start=(c == 0), stop=False), wait=wait)
        for s in range(NPS):
            tgt = p.op("tensor", lambda e, s=s: e.matmul(
                ffnb[s], oner_sb[:], fb_sb[0:1, s * 512:s * 512 + PSW],
                start=False, stop=True), inc="PE" if s == NPS - 1 else None)
        pe_ffn[i] = tgt

    def emit_back_tail(i):
        nonlocal sc_r4_prev, out_prev
        par = i % 2
        wait = [("PE", pe_ffn[i]), ("DV", dv_ln3[i]),
                ("DV", dv_x3ts.get(i - 1, 0))]
        if i < 2:
            wait += [("PE", B_PE)] + [(f"WS{k}", ws_tot[k]) for k in range(4)]
        w1 = min(1024, DM)
        s2_copies = [(s2buf[:, 0:w1], psC[:, 0:w1])]
        if NPS > 2:
            s2_copies.append((s2buf[:, 1024:1536], psD[:]))
        if NPS > 3:
            s2_copies.append((s2buf[:, 1536:2048], psE[:]))
        last = None
        for ci, (d, srcp) in enumerate(s2_copies):
            if ci > 0:
                p.drain("scalar")
            last = p.op("scalar", lambda e, d=d, srcp=srcp: e.copy(d, srcp),
                        wait=(wait if ci == 0 else ()),
                        inc="SC" if ci == len(s2_copies) - 1 else None)
        sc_s2[i] = last
        aggr4 = emit_ln_stats(s2buf, 2, [("SC", sc_s2[i])], war_sc=sc_r4_prev)
        sc_r4 = emit_rstd(2, aggr4, war_dv=dv_x3ts.get(i - 1, 0))
        sc_r4_prev = sc_r4
        p.drain("vector")
        p.op("vector", lambda e: e.tensor_tensor(
            corr[:, 2:3], mv2[2][:, 0:1], rstd[2][:, 0:1], ALU.mult),
            wait=[("SC", sc_r4)])
        p.drain("vector")
        p.op("vector", lambda e, par=par: e.tensor_tensor(
            corr[:, 2:3], corr[:, 2:3], corr[:, par:par + 1], ALU.add))
        p.drain("vector")
        p.op("vector", lambda e: e.tensor_scalar(
            corr[:, 2:3], corr[:, 2:3], -1.0, None, ALU.mult))
        p.drain("vector")
        wait = [("OUT0", out_prev)]
        dv_x3ts[i] = p.op("vector", lambda e: e.tensor_scalar(
            x3buf, s2buf, rstd[2][:, 0:1], corr[:, 2:3],
            ALU.mult, ALU.add), wait=wait, inc="DV")
        p.drain("vector")
        gp_x3[i] = p.op("vector", lambda e, par=par: e.tensor_tensor(
            x3buf, x3buf, xin[par], ALU.add), inc="DV")
        out_prev = p.op("gpsimd", lambda e, i=i: e.dma_start(
            out_h.ap()[i * 128:(i + 1) * 128, :], x3buf),
            wait=[("DV", gp_x3[i])], inc="OUT0", incn=16)

    B0_PE = p.cnt("PE")
    B0_DV = p.cnt("DV")
    B0_SC = p.cnt("SC")
    B0_GP = p.cnt("GP")
    for t in range(NT):
        emit_bw_tile(t)
        if t == 4 and NT >= 12:
            emit_front_pe(0)
    emit_bw_evacs(NT - 1)

    B_PE = p.cnt("PE")
    B_DV = p.cnt("DV")
    B_SC = p.cnt("SC")
    B_GP = p.cnt("GP")
    fw_tot2 = [p.cnt("FW0"), p.cnt("FW1")]

    pre = 1 if NT >= 12 else 0
    for i in range(NT):
        if i >= pre:
            emit_front_pe(i)
        if i > 0:
            emit_back_tr_ffn(i - 1)
        emit_front_ln2(i)
        emit_front_chain(i)
        if i > 0:
            emit_back_tail(i - 1)
    emit_back_tr_ffn(NT - 1)
    emit_back_tail(NT - 1)

    p.op("sync", lambda e: e.nop(),
         wait=[("OUT0", p.cnt("OUT0"))])

    import contextlib
    with contextlib.ExitStack() as es:
        blk = es.enter_context(nc.Block())
        for eng in Plan.ENGS:
            getattr(blk, eng)(lambda e, eng=eng: [f(e) for f in p.plans[eng]])
    nc.finalize()
    return nc


_SEQ, _DM, _H, _B = 2048, 2048, 16, 8
_nc_cache = {}


def _get_nc():
    key = (_SEQ, _DM, _H)
    if key not in _nc_cache:
        _nc_cache[key] = build_program(*key)
    return _nc_cache[key]


def kernel(L0_attn_x, L0_attn_W_Q, L0_attn_W_K, L0_attn_W_V, L0_attn_W_O,
           L0_attn_b_Q, L0_attn_b_K, L0_attn_b_V, L0_attn_b_O,
           L0_ffn_weight, L0_ffn_bias):
    x = np.ascontiguousarray(np.asarray(L0_attn_x, dtype=np.float32))
    common = dict(
        wq=np.ascontiguousarray(np.asarray(L0_attn_W_Q, np.float32)),
        wk=np.ascontiguousarray(np.asarray(L0_attn_W_K, np.float32)),
        wv=np.ascontiguousarray(np.asarray(L0_attn_W_V, np.float32)),
        wo=np.ascontiguousarray(np.asarray(L0_attn_W_O, np.float32)),
        bqkvT=np.ascontiguousarray(np.concatenate(
            [np.asarray(L0_attn_b_Q, np.float32).T,
             np.asarray(L0_attn_b_K, np.float32).T,
             np.asarray(L0_attn_b_V, np.float32).T], axis=1)),
        bo=np.ascontiguousarray(np.asarray(L0_attn_b_O, np.float32)),
        fw=np.ascontiguousarray(np.asarray(L0_ffn_weight, np.float32)),
        fb=np.ascontiguousarray(np.asarray(L0_ffn_bias, np.float32)),
        eye=np.eye(128, dtype=BF),
        epsb=np.full((128, 1), 1e-5, np.float32),
        lncb=np.full((1, 1), math.log(1.0 / math.sqrt(128.0)), np.float32),
        tri=np.triu(np.ones((128, 128), np.float32)).astype(BF),
        onec=np.ones((128, 1), dtype=BF),
        oner=np.ones((1, 128), dtype=BF),
        onem=np.ones((128, 128), dtype=BF),
    )
    nc = _get_nc()
    in_maps = [dict(x=np.ascontiguousarray(x[b]), **common) for b in range(_B)]
    res = run_bass_kernel_spmd(nc, in_maps, core_ids=list(range(_B)))
    return np.stack([res.results[b]["out"] for b in range(_B)], axis=0)


# version 33
# speedup vs baseline: 1.0060x; 1.0060x over previous
import math
import numpy as np
import ml_dtypes
import concourse.bass as bass
import concourse.mybir as mybir
from concourse.bass_utils import run_bass_kernel_spmd

F32 = mybir.dt.float32
BF16 = mybir.dt.bfloat16
AF = mybir.ActivationFunctionType
ALU = mybir.AluOpType
AX = mybir.AxisListType
BF = ml_dtypes.bfloat16

LN_EPS = 1e-5


class Plan:
    ENGS = ("sync", "tensor", "vector", "scalar", "gpsimd")

    def __init__(self, nc):
        self.nc = nc
        self.plans = {e: [] for e in self.ENGS}
        self.sems = {}

    def sem(self, name):
        h = self.nc.alloc_semaphore(name)
        self.sems[name] = [h, 0]
        return name

    def cnt(self, name):
        return self.sems[name][1]

    def op(self, eng, fn, wait=(), inc=None, incn=1, lowp=None):
        tgt = None
        if inc is not None:
            self.sems[inc][1] += incn
            tgt = self.sems[inc][1]
        sems = self.sems
        nc = self.nc
        wait = tuple((s, v) for (s, v) in wait if v and v > 0)

        def thunk(e, fn=fn, wait=wait, inc=inc, incn=incn, lowp=lowp):
            for (s, v) in wait:
                e.wait_ge(sems[s][0], v)
            if lowp is not None:
                with nc.allow_low_precision(reason=lowp):
                    ins = fn(e)
            else:
                ins = fn(e)
            if inc is not None:
                ins.then_inc(sems[inc][0], incn)

        self.plans[eng].append(thunk)
        return tgt

    def drain(self, eng):
        self.plans[eng].append(lambda e: e.drain())


def build_program(SEQ, DM, H):
    assert SEQ == DM
    NCH = DM // 128
    NT = SEQ // 128
    SLW = min(512, SEQ)
    NSL = SEQ // SLW
    STW = min(512, DM)
    NST = DM // STW
    TPG = min(4, NT)
    NG = NT // TPG
    GQ = TPG * 128
    NPS = DM // 512 if DM >= 512 else 1
    PSW = min(512, DM)
    C_ATTN = 1.0 / math.sqrt(128.0)

    nc = bass.Bass("TRN2", target_bir_lowering=False)
    p = Plan(nc)

    x_h = nc.dram_tensor("x", [SEQ, DM], F32, kind="ExternalInput")
    wq_h = nc.dram_tensor("wq", [H, DM, 128], F32, kind="ExternalInput")
    wk_h = nc.dram_tensor("wk", [H, DM, 128], F32, kind="ExternalInput")
    wv_h = nc.dram_tensor("wv", [H, DM, 128], F32, kind="ExternalInput")
    wo_h = nc.dram_tensor("wo", [H, 128, DM], F32, kind="ExternalInput")
    bqkvT_h = nc.dram_tensor("bqkvT", [128, 3 * H], F32, kind="ExternalInput")
    bo_h = nc.dram_tensor("bo", [DM], F32, kind="ExternalInput")
    fw_h = nc.dram_tensor("fw", [DM, DM], F32, kind="ExternalInput")
    fb_h = nc.dram_tensor("fb", [DM], F32, kind="ExternalInput")
    eye_h = nc.dram_tensor("eye", [128, 128], BF16, kind="ExternalInput")
    tri_h = nc.dram_tensor("tri", [128, 128], BF16, kind="ExternalInput")
    onec_h = nc.dram_tensor("onec", [128, 1], BF16, kind="ExternalInput")
    oner_h = nc.dram_tensor("oner", [1, 128], BF16, kind="ExternalInput")
    onem_h = nc.dram_tensor("onem", [128, 128], BF16, kind="ExternalInput")
    epsb_h = nc.dram_tensor("epsb", [128, 1], F32, kind="ExternalInput")
    lncb_h = nc.dram_tensor("lncb", [1, 1], F32, kind="ExternalInput")
    out_h = nc.dram_tensor("out", [SEQ, DM], F32, kind="ExternalOutput")
    zs_h = nc.dram_tensor("zs", [NT, H, 128, 128], BF16)

    A = nc.alloc_sbuf_tensor("arena_a", [128, NCH * SEQ], BF16)
    Bz = nc.alloc_sbuf_tensor("arena_b", [128, H * DM], BF16)
    wbuf = nc.alloc_sbuf_tensor("wbuf", [128, 4 * NCH * 128], BF16)
    qt_sb = nc.alloc_sbuf_tensor("qt", [128, SEQ], BF16)
    kt_sb = nc.alloc_sbuf_tensor("kt", [128, SEQ], BF16)
    v_sb = nc.alloc_sbuf_tensor("v", [128, SEQ], BF16)
    ptil = nc.alloc_sbuf_tensor("ptil", [128, max(NT * GQ, 4 * DM)], BF16)
    C2 = nc.alloc_sbuf_tensor("c2", [128, max(6 * DM, 4 * GQ + SEQ + 4 * DM)], BF16)
    eye_sb = nc.alloc_sbuf_tensor("eye_sb", [128, 128], BF16)
    tri_sb = nc.alloc_sbuf_tensor("tri_sb", [128, 128], BF16)
    onec_sb = nc.alloc_sbuf_tensor("onec_sb", [128, 1], BF16)
    oner_sb = nc.alloc_sbuf_tensor("oner_sb", [1, 128], BF16)
    bqkvT_sb = nc.alloc_sbuf_tensor("bqkvT_sb", [128, 3 * H], F32)
    bo_sb = nc.alloc_sbuf_tensor("bo_sb", [1, DM], BF16)
    fb_sb = nc.alloc_sbuf_tensor("fb_sb", [1, DM], BF16)
    st6 = [nc.alloc_sbuf_tensor(f"st6_{i}", [128, 6 * NST], F32) for i in range(3)]
    mv2 = [nc.alloc_sbuf_tensor(f"mv2_{i}", [128, 2], F32) for i in range(3)]
    rstd = [nc.alloc_sbuf_tensor(f"rstd_{i}", [128, 1], F32) for i in range(3)]
    corr = nc.alloc_sbuf_tensor("corr", [128, 3], F32)
    onem_sb = nc.alloc_sbuf_tensor("onem_sb", [128, 128], BF16)
    mbias = nc.alloc_sbuf_tensor("mbias", [128, 1], F32)
    srow = nc.alloc_sbuf_tensor("srow", [1, 24], F32)
    nm_bf = nc.alloc_sbuf_tensor("nm_bf", [1, 1], BF16)
    asum = nc.alloc_sbuf_tensor("asum", [128, 1], F32)
    twoal = nc.alloc_sbuf_tensor("twoal", [128, 1], F32)
    negal = nc.alloc_sbuf_tensor("negal", [128, 1], F32)
    epsb_sb = nc.alloc_sbuf_tensor("epsb_sb", [128, 1], F32)
    lncb_sb = nc.alloc_sbuf_tensor("lncb_sb", [1, 1], F32)

    ln1t3 = A.rearrange("p (c s) -> p c s", c=NCH)
    bwt3 = ln1t3
    wslot = [wbuf[:, s * NCH * 128:(s + 1) * NCH * 128] for s in range(4)]
    x3v = wbuf.bitcast(F32)
    x3buf = x3v[:, 0:DM]
    s2buf = x3v[:, DM:2 * DM]
    pslot = [ptil[:, j * GQ:(j + 1) * GQ] for j in range(NT)]
    vstage = ptil[:, 0:SEQ]
    xinv = ptil.bitcast(F32)
    xin = [xinv[:, i * DM:(i + 1) * DM] for i in range(2)]
    c2f = C2.bitcast(F32)
    xa = [c2f[:, i * DM:(i + 1) * DM] for i in range(3)] + \
        [ptil.bitcast(F32)[:, SEQ:SEQ + DM]]
    zstage = [C2[:, i * GQ:(i + 1) * GQ] for i in range(2)]
    rbc = [C2[:, (2 + i) * GQ:(3 + i) * GQ] for i in range(2)]
    sqb = C2[:, 4 * GQ:4 * GQ + SEQ]
    fw_off = (4 * GQ + SEQ) // 2
    fwst = [c2f[:, fw_off + i * DM:fw_off + (i + 1) * DM] for i in range(2)]
    s1 = c2f[:, 0:DM]
    x2 = c2f[:, DM:2 * DM]
    ln3 = C2[:, 4 * DM:5 * DM]
    ln3t = C2[:, 5 * DM:5 * DM + NCH * 128]

    psA = nc.alloc_psum_tensor("psA", [128, 1024], F32)
    psB = nc.alloc_psum_tensor("psB", [128, 1024], F32)
    psC = nc.alloc_psum_tensor("psC", [128, 1024], F32)
    psD = nc.alloc_psum_tensor("psD", [128, 512], F32)
    psE = nc.alloc_psum_tensor("psE", [128, 512], F32)
    psA_bf = psA.bitcast(BF16)
    psB_bf = psB.bitcast(BF16)
    psC_bf = psC.bitcast(BF16)
    psD_bf = psD.bitcast(BF16)
    psE_bf = psE.bitcast(BF16)
    schalf = [psA[:, 0:GQ], psA[:, 512:512 + GQ],
              psB[:, 0:GQ], psB[:, 512:512 + GQ]]

    for s in ("CONST", "CONSTG", "ZW0", "ZW1", "OUT0", "OUT1", "WO", "PE",
              "DV", "SC", "GP"):
        p.sem(s)
    XAs = [p.sem(f"XA{i}") for i in range(4)]
    XCs = [p.sem(f"XC{i}") for i in range(2)]
    ZRs = [p.sem(f"ZR{i}") for i in range(2)]
    FWs = [p.sem(f"FW{i}") for i in range(2)]
    WSs = [p.sem(f"WS{i}") for i in range(4)]
    ZWs = ("ZW0", "ZW1")
    OUTs = ("OUT0", "OUT1")

    for dst, src in ((eye_sb, eye_h), (tri_sb, tri_h), (onec_sb, onec_h),
                     (oner_sb, oner_h), (onem_sb, onem_h), (bqkvT_sb, bqkvT_h),
                     (epsb_sb, epsb_h), (lncb_sb, lncb_h)):
        p.op("sync", lambda e, d=dst, s=src: e.dma_start(d[:], s.ap()),
             inc="CONST", incn=16)
    const_t = p.cnt("CONST")
    p.op("gpsimd", lambda e: e.dma_start(
        bo_sb[:], bo_h.ap().rearrange("(o d) -> o d", o=1)), inc="CONSTG", incn=16)
    p.op("gpsimd", lambda e: e.dma_start(
        fb_sb[:], fb_h.ap().rearrange("(o d) -> o d", o=1)), inc="CONSTG", incn=16)
    constg_t = p.cnt("CONSTG")

    w_handles = (wq_h, wk_h, wv_h)
    ws_t = {}
    pe_w_done = {}

    def emit_w_load(h, wi):
        slot = (3 * h + wi) % 4
        wait = []
        m = 3 * h + wi - 4
        if m >= 0:
            hu, iu = divmod(m, 3)
            wait.append(("PE", pe_w_done[(hu, iu)]))
        src = w_handles[wi].ap()[h].rearrange("(c q) e -> q c e", q=128)
        ws_t[(h, wi)] = p.op(
            "gpsimd", lambda e, d=wslot[slot], s=src: e.dma_start(d, s),
            wait=wait, inc=WSs[slot], incn=16)

    for wi in range(3):
        emit_w_load(0, wi)

    def emit_ln_stats(src_ap, par, wait, war_sc=0):
        for g in range(NST):
            w = list(wait) if g == 0 else []
            if g == 0 and war_sc:
                w.append(("SC", war_sc))
            p.op("vector", lambda e, g=g, a=src_ap, pr=par: e.bn_stats(
                st6[pr][:, 6 * g:6 * g + 6], a[:, STW * g:STW * (g + 1)]), wait=w)
        p.drain("vector")
        return p.op("vector", lambda e, pr=par: e.bn_aggr(mv2[pr][:], st6[pr][:]),
                    inc="DV")

    def emit_rstd(par, wait_dv, war_dv=0, extra=()):
        w = [("DV", wait_dv)] + [t for t in extra if t[1]]
        if war_dv:
            w.append(("DV", war_dv))
        w.append(("CONST", const_t))
        p.op("scalar", lambda e, pr=par: e.activation(
            rstd[pr][:], mv2[pr][:, 1:2], AF.Ln, bias=epsb_sb[:, 0:1],
            scale=1.0), wait=w)
        p.drain("scalar")
        return p.op("scalar", lambda e, pr=par: e.activation(
            rstd[pr][:], rstd[pr][:], AF.Exp, bias=0.0, scale=-0.5), inc="SC")

    dv_apply = {}
    dv_evacA = {}
    pe_trA = {}
    sc_rstdA = {}
    ln1bf = [v_sb, ptil[:, SEQ:2 * SEQ]]
    xa_tot = [0, 0, 0, 0]

    def emit_A_head(t):
        slot = t % 4
        par = t % 3
        wait = [("DV", dv_apply[t - 4])] if t - 4 in dv_apply else []
        xt = p.op(("sync", "gpsimd")[slot % 2], lambda e, sl=slot, t=t:
                  e.dma_start(xa[sl], x_h.ap()[t * 128:(t + 1) * 128, :]),
                  wait=wait, inc=XAs[slot], incn=16)
        xa_tot[slot] = xt
        aggr = emit_ln_stats(xa[slot], par, [(XAs[slot], xt)],
                             war_sc=sc_rstdA.get(t - 3, 0))
        sc_rstdA[t] = emit_rstd(par, aggr, war_dv=dv_apply.get(t - 3, 0))

    def emit_A_tail(t):
        par = t % 3
        slot = t % 4
        bpar = t % 2
        wait = [("SC", sc_rstdA[t])]
        if t - 2 in pe_trA:
            wait.append(("PE", pe_trA[t - 2]))
        p.drain("vector")
        hn2 = max(NCH // 2, 1)
        hw = hn2 * 128
        dv_apply_h = [0, 0]
        dv_apply_h[0] = p.op("vector", lambda e, pr=par, sl=slot, bp=bpar,
                             hw=hw: e.tensor_scalar(
            ln1bf[bp][:, 0:hw], xa[sl][:, 0:hw], mv2[pr][:, 0:1],
            rstd[pr][:, 0:1], ALU.subtract, ALU.mult), wait=wait, inc="DV")
        dv_apply_h[1] = dv_apply[t] = p.op(
            "vector", lambda e, pr=par, sl=slot, bp=bpar, hw=hw:
            e.tensor_scalar(
                ln1bf[bp][:, hw:NCH * 128], xa[sl][:, hw:NCH * 128],
                mv2[pr][:, 0:1], rstd[pr][:, 0:1],
                ALU.subtract, ALU.mult), inc="DV") \
            if NCH > 1 else dv_apply_h[0]
        if NCH == 1:
            dv_apply[t] = dv_apply_h[0]
        last_sc = 0
        for half in range(2 if NCH > 1 else 1):
            b = (psD_bf, psE_bf)[half]
            c0 = half * hn2
            c1 = NCH if half else hn2
            wait = [("DV", dv_apply_h[half]), ("CONST", const_t),
                    ("SC", sc_evacA.get((t - 1, half), 0))]
            for c in range(c0, c1):
                w = wait if c == c0 else ()
                tgt = p.op("tensor", lambda e, c=c, b=b, c0=c0, bp=bpar:
                           e.transpose(b[:, (c - c0) * 128:(c - c0 + 1) * 128],
                                       ln1bf[bp][:, c * 128:(c + 1) * 128],
                                       eye_sb[:]), wait=w,
                           inc="PE" if c == c1 - 1 else None)
            pe_trA[t] = tgt
            p.drain("scalar")
            last_sc = p.op("scalar", lambda e, t=t, b=b, c0=c0, c1=c1: e.copy(
                ln1t3[:, c0:c1, t * 128:(t + 1) * 128],
                b[:, 0:(c1 - c0) * 128].rearrange("p (c q) -> p c q", c=c1 - c0)),
                wait=[("PE", pe_trA[t])], inc="SC")
            sc_evacA[(t, half)] = last_sc
        sc_evacA[t] = last_sc
        dv_evacA[t] = 0

    qkv_dst = (qt_sb[:], kt_sb[:], vstage)
    psbuf_last_rd = [0, 0, 0, 0]
    ptil_slot_av = [0] * NT
    pe_av_all = {}
    dv_zevac = {}
    zw_par_t = [0, 0]
    dv_bank_free = 0
    pe_norm_last = 0
    dv_vch = {}
    dv_vch_lo = {}
    gidx = 0
    sc_rinv_prev = 0
    dv_mbias_prev = 0
    pe_mb_prev = 0
    dv_red_last = 0
    dv_mx_d = {1: 0}
    dv_sq_d = {}
    dv_red_d = {}
    dv_nm_cur = 0
    dv_mbias = 0
    sc_lns = 0
    rbc_war = [0, 0]
    sc_after_masks = 0

    SET0 = None
    dv_qkv_evac_d = {}
    dv_qkv_evac_s = {}
    pe_w_done_d = {}

    sc_bw_evac_half = [0, 0]
    sc_sign_d = {}
    dv_fw_free = {}
    pe_bw_last = 0
    pe_bw_par = [0, 0]
    pe_bw_tr = {}

    def emit_w_mms(h, wi, only_slice=None, defer_evac=False):
        nonlocal dv_bank_free
        bankset = [[psD[:, 0:SLW], psE[:, 0:SLW], psC[:, 0:SLW],
                    psC[:, 512:512 + SLW]],
                   [psA[:, 0:SLW], psA[:, 512:512 + SLW], psB[:, 0:SLW],
                    psB[:, 512:512 + SLW]]]
        sidx = 0 if wi == 1 else 1
        banks = bankset[sidx]
        slot = (3 * h + wi) % 4
        dst = (qt_sb[:], kt_sb[:], vstage)[wi]
        tpsl = NT // NSL
        pe_mm = 0
        deferred = []
        if only_slice is not None:
            slices = [(only_slice, 1)]
        else:
            slices = [(sq0, min(4, NSL - sq0)) for sq0 in range(0, NSL, 4)]
        for sq0, nq in slices:
            for c in range(NCH):
                wait = []
                if c == 0:
                    wait = [(WSs[slot], ws_t[(h, wi)])]
                    if only_slice is not None:
                        wait.append(("DV", dv_qkv_evac_d.get((h, wi), 0)))
                    if sidx == 0:
                        wait += [("DV", dv_zevac.get(gidx - 1, 0)),
                                 ("DV", dv_mbias_prev),
                                 ("SC", sc_rinv_prev),
                                 ("SC", max(sc_bw_evac_half)),
                                 ("DV", dv_qkv_evac_d.get((h, 0), dv_bank_free))]
                    else:
                        wait += [("SC", max(psbuf_last_rd)),
                                 ("DV", dv_vch.get(h - 1, 0)),
                                 ("DV", dv_red_last)]
                        if wi == 2:
                            wait.append(("DV", dv_qkv_evac_d.get((h, 0), 0)))
                for k in range(nq):
                    s = sq0 + k
                    w = list(wait) if k == 0 else []
                    if c == 0:
                        w.append(("SC", sc_evacA[min((s + 1) * tpsl, NT) - 1]))
                    w = w or ()
                    bidx = (2 + s % 2) if (only_slice is not None and wi == 1) \
                        else (s % 4)
                    tgt = p.op(
                        "tensor", lambda e, c=c, s=s, sl=slot, banks=banks,
                        bidx=bidx:
                        e.matmul(banks[bidx],
                                 wslot[sl][:, c * 128:(c + 1) * 128],
                                 ln1t3[:, c, s * SLW:(s + 1) * SLW],
                                 start=(c == 0), stop=(c == NCH - 1)),
                        wait=w,
                        inc="PE" if (c == NCH - 1 and k == nq - 1) else None)
            pe_mm = tgt

            def do_evacs(sq0=sq0, nq=nq, pe_mm=pe_mm, banks=banks):
                nonlocal dv_bank_free
                for k in range(nq):
                    s = sq0 + k
                    bidx = (2 + s % 2) if (only_slice is not None and
                                           wi == 1) else (s % 4)
                    dv_bank_free = p.op(
                        "vector", lambda e, s=s, d=dst, wi=wi, h=h,
                        banks=banks, bidx=bidx:
                        e.tensor_scalar(d[:, s * SLW:(s + 1) * SLW],
                                        banks[bidx],
                                        bqkvT_sb[:, wi * H + h:
                                                 wi * H + h + 1],
                                        None, ALU.add),
                        wait=[("PE", pe_mm)], inc="DV")
                    dv_qkv_evac_s[(h, wi, s)] = dv_bank_free
            if defer_evac:
                deferred.append(do_evacs)
            else:
                do_evacs()
        if (h, wi) in pe_w_done_d:
            pe_mm = max(pe_mm, pe_w_done_d[(h, wi)])
        pe_w_done_d[(h, wi)] = pe_mm
        pe_w_done[(h, wi)] = pe_mm

        def finish():
            nonlocal dv_bank_free
            for f in deferred:
                f()
            cur = dv_qkv_evac_d.get((h, wi), 0)
            dv_qkv_evac_d[(h, wi)] = max(cur, dv_bank_free)
        if defer_evac:
            return finish
        finish()

    sc_evacA = {}
    wA_flush = {}
    for t in range(NT + 2):
        if t < NT:
            emit_A_head(t)
        if t - 2 >= 0:
            emit_A_tail(t - 2)
            tp = t - 2
            for f in wA_flush.pop(tp, ()):
                f()
            if (tp + 1) % (NT // NSL) == 0 and H > 0:
                s = (tp + 1) // (NT // NSL) - 1
                flq = emit_w_mms(0, 0, only_slice=s, defer_evac=True)
                flk = emit_w_mms(0, 1, only_slice=s, defer_evac=True)
                flq()
                flv = emit_w_mms(0, 2, only_slice=s, defer_evac=True)
                wA_flush[tp + 3] = [flk, flv]
    for fl in sorted(wA_flush):
        for f in wA_flush[fl]:
            f()

    sc_ln1t_all = sc_evacA[NT - 1]

    def emit_bw_evacs(t):
        hn2 = max(NCH // 2, 1)
        for half in range(2 if NCH > 1 else 1):
            b = (psD_bf, psE_bf)[half]
            c0 = half * hn2
            c1 = NCH if half else hn2
            sc_bw_evac_half[half] = p.op(
                "scalar", lambda e, t=t, b=b, c0=c0, c1=c1: e.copy(
                    bwt3[:, c0:c1, t * 128:(t + 1) * 128],
                    b[:, 0:(c1 - c0) * 128].rearrange("p (c q) -> p c q",
                                                      c=c1 - c0)),
                wait=[("PE", pe_bw_tr[(t, half)])], inc="SC")

    def emit_bw_tile(t):
        nonlocal pe_bw_last
        fpar = t % 2
        stg = (v_sb, kt_sb)[fpar]
        wait = [("SC", sc_ln1t_all),
                (XAs[1], xa_tot[1]), (XAs[2], xa_tot[2])] if t < 2 else []
        if t - 2 in dv_fw_free:
            wait = [("DV", dv_fw_free[t - 2]), ("SC", sc_sign_d[t - 2])]
        fw_t = p.op("gpsimd", lambda e, t=t, fpar=fpar: e.dma_start(
            fwst[fpar], fw_h.ap()[t * 128:(t + 1) * 128, :]),
            wait=wait, inc=FWs[fpar], incn=16)
        dv_as = p.op("vector", lambda e, fpar=fpar: e.reduce_sum(
            asum[:], fwst[fpar], AX.X, apply_absolute_value=True),
            wait=[(FWs[fpar], fw_t)], inc="DV")
        p.drain("vector")
        dv_al = p.op("vector", lambda e: e.tensor_scalar(
            twoal[:], asum[:], 1.0 / DM, None, ALU.mult), inc="DV")
        dv_fw_free[t] = dv_as
        w2 = [(FWs[fpar], fw_t), ("PE", pe_bw_par[fpar])]
        if t < 2:
            w2.append(("PE", pe_av_all[H - 1]))
        sc_sign_d[t] = p.op("scalar", lambda e, fpar=fpar, stg=stg:
                            e.activation(stg[:], fwst[fpar], AF.Sign,
                                         bias=0.0, scale=1.0),
                            wait=w2, inc="SC",
                            lowp="binarized weight is a bf16 matmul operand")
        if t >= 1:
            emit_bw_evacs(t - 1)
        dv_bw = p.op("vector", lambda e, stg=stg: e.tensor_scalar(
            stg[:], stg[:], twoal[:, 0:1], None, ALU.mult),
            wait=[("SC", sc_sign_d[t]), ("DV", dv_al)], inc="DV",
            lowp="binarized weight is a bf16 matmul operand")
        hn2 = max(NCH // 2, 1)
        for half in range(2 if NCH > 1 else 1):
            b = (psD_bf, psE_bf)[half]
            c0 = half * hn2
            c1 = NCH if half else hn2
            wait = [("DV", dv_bw), ("SC", sc_bw_evac_half[half]),
                    ("DV", dv_zevac.get(gidx - 1, 0))]
            for c in range(c0, c1):
                w3 = wait if c == c0 else ()
                tgt = p.op("tensor", lambda e, c=c, b=b, c0=c0, stg=stg:
                           e.transpose(b[:, (c - c0) * 128:(c - c0 + 1) * 128],
                                       stg[:, c * 128:(c + 1) * 128],
                                       eye_sb[:]), wait=w3,
                           inc="PE" if c == c1 - 1 else None)
            pe_bw_tr[(t, half)] = tgt
            pe_bw_last = tgt
        pe_bw_par[fpar] = pe_bw_last

    def emit_mh_sq(which, h):
        src = (qt_sb, kt_sb)[which]
        for s in range(NSL):
            w0 = [("DV", dv_qkv_evac_s.get((h, which, s),
                                           dv_qkv_evac_d[(h, which)]))]
            if s == 0:
                w0.append(("PE", pe_norm_last))
                if which == 0:
                    w0 += [("PE", pe_mb_prev), ("PE", pe_bw_last)]
            dv_sq_d[(which, s)] = p.op(
                "vector", lambda e, s=s, src=src: e.tensor_mul(
                    sqb[:, s * SLW:(s + 1) * SLW],
                    src[:, s * SLW:(s + 1) * SLW],
                    src[:, s * SLW:(s + 1) * SLW]), wait=w0, inc="DV")

    def emit_mh_mms(which, h):
        nonlocal pe_norm_last, dv_red_last, sc_lns
        for s in range(NSL):
            hb = s % 4
            wait = [("DV", dv_sq_d[(which, s)]), ("CONST", const_t),
                    ("SC", psbuf_last_rd[hb])]
            if which == 1:
                wait.append(("DV", dv_red_d[(0, s)]))
                wait.append(("DV", dv_qkv_evac_s.get(
                    (h, 2, s), dv_qkv_evac_d[(h, 2)])))
            elif s == 0 and h == 0:
                wait.append(("DV", dv_qkv_evac_d[(h, 2)]))
            pe_norm_last = p.op("tensor", lambda e, s=s, hb=hb: e.matmul(
                [psA[:, 0:SLW], psA[:, 512:512 + SLW], psB[:, 0:SLW],
                 psB[:, 512:512 + SLW]][hb], onem_sb[:],
                sqb[:, s * SLW:(s + 1) * SLW], start=True, stop=True),
                wait=wait, inc="PE")
            dv_red_d[(which, s)] = p.op(
                "vector", lambda e, s=s, hb=hb, wh=which: e.reduce_max(
                    srow[0:1, wh * 8 + s:wh * 8 + s + 1],
                    [psA[0:1, 0:SLW], psA[0:1, 512:512 + SLW],
                     psB[0:1, 0:SLW], psB[0:1, 512:512 + SLW]][hb], AX.X),
                wait=[("PE", pe_norm_last)], inc="DV")
        p.drain("vector")
        dv_mx_d[which] = p.op("vector", lambda e, wh=which: e.reduce_max(
            srow[0:1, 16 + wh:17 + wh], srow[0:1, wh * 8:wh * 8 + NSL],
            AX.X), inc="DV")
        dv_red_last = dv_mx_d[which]
        p.drain("scalar")
        t_ln = p.op("scalar", lambda e, wh=which: e.activation(
            srow[0:1, 18 + wh:19 + wh], srow[0:1, 16 + wh:17 + wh], AF.Ln,
            bias=0.0, scale=1.0), wait=[("DV", dv_mx_d[which])],
            inc="SC" if which == 1 else None)
        if which == 1:
            sc_lns = t_ln

    def emit_mh_tail(h):
        nonlocal dv_nm_cur
        p.drain("vector")
        dv_add = p.op("vector", lambda e: e.tensor_tensor(
            srow[0:1, 20:21], srow[0:1, 18:19], srow[0:1, 19:20], ALU.add),
            wait=[("SC", sc_lns)], inc="DV")
        sc_M = p.op("scalar", lambda e: e.activation(
            srow[0:1, 21:22], srow[0:1, 20:21], AF.Exp,
            bias=lncb_sb[0:1, 0:1], scale=0.5),
            wait=[("DV", dv_add)], inc="SC")
        p.drain("vector")
        dv_nm_cur = p.op("vector", lambda e: e.tensor_scalar(
            nm_bf[:], srow[0:1, 21:22], -1.0, None, ALU.mult),
            wait=[("SC", sc_M)], inc="DV", lowp="softmax shift operand; "
            "softmax is shift-invariant so the value need not be exact")

    def emit_mb(h):
        nonlocal pe_mb_prev, dv_mbias, dv_mbias_prev
        pe_mb = p.op("tensor", lambda e: e.matmul(
            psC[:, 0:1], oner_sb[:], nm_bf[:], start=True, stop=True),
            wait=[("DV", dv_nm_cur), ("DV", dv_qkv_evac_d[(h, 1)]),
                  ("SC", sc_rinv_prev)], inc="PE")
        pe_mb_prev = pe_mb
        dv_mbias = p.op("vector", lambda e: e.tensor_copy(
            mbias[:], psC[:, 0:1]), wait=[("PE", pe_mb)], inc="DV")
        dv_mbias_prev = dv_mbias

    for h in range(H):
        if h > 0:
            emit_w_mms(h, 1)
            emit_mh_sq(1, h)
            emit_w_mms(h, 2)
        dv_qkv_evac = [dv_qkv_evac_d[(h, wi)] for wi in range(3)]
        dv_qkv_all = max(dv_qkv_evac)
        if h + 1 < H:
            for wi in range(3):
                emit_w_load(h + 1, wi)
        wo_start = max(0, H - 4)
        if h >= wo_start:
            win = H - wo_start
            per = (H + win - 1) // win
            hh0 = (h - wo_start) * per
            for hh in range(hh0, min(hh0 + per, H)):
                p.op("gpsimd", lambda e, hh=hh: e.dma_start(
                    Bz[:, hh * DM:(hh + 1) * DM], wo_h.ap()[hh]),
                    inc="WO", incn=16)

        if h == 0:
            emit_mh_sq(0, 0)
            emit_mh_mms(0, 0)
            emit_mh_sq(1, 0)
        emit_mh_mms(1, h)
        emit_mh_tail(h)

        wait = [("DV", dv_qkv_all), ("DV", dv_mx_d[1])]
        nh = NT // 2
        pe_vtr_half = 0
        for t in range(NT):
            w = wait if t == 0 else ()
            tgt = p.op("tensor", lambda e, t=t: e.transpose(
                psA_bf[:, t * 128:(t + 1) * 128], vstage[:, t * 128:(t + 1) * 128],
                eye_sb[:]), wait=w,
                inc="PE" if t in (nh - 1, NT - 1) else None)
            if t == nh - 1:
                pe_vtr_half = tgt
        pe_vtr = tgt
        for j in range(max(SEQ // GQ, 1)):
            ptil_slot_av[j] = max(ptil_slot_av[j], pe_vtr)
        dv_vch_lo[h] = p.op("vector", lambda e, nh=nh: e.tensor_copy(
            v_sb[:, 0:nh * 128], psA_bf[:, 0:nh * 128]),
            wait=[("PE", pe_vtr_half)], inc="DV")
        dv_vch[h] = p.op("vector", lambda e, nh=nh: e.tensor_copy(
            v_sb[:, nh * 128:NT * 128], psA_bf[:, nh * 128:NT * 128]),
            wait=[("PE", pe_vtr)], inc="DV")

        estep = 2 if GQ == 512 else 1

        def emit_score(g, j, pe_sc):
            hb = j % 4
            wait = [("SC", psbuf_last_rd[hb])]
            split_ok = NT * 128 == 2048
            if j == 0:
                wait.append(("DV", dv_qkv_all))
                wait.append(("DV", dv_vch_lo[h] if (g == 0 and split_ok)
                             else dv_vch[h]))
                if g == 0:
                    wait.append(("DV", dv_mx_d[1]))
            elif g == 0 and j == 1 and split_ok:
                wait.append(("DV", dv_vch[h]))
            pe_sc[j] = p.op("tensor", lambda e, j=j, g=g, hb=hb: e.matmul(
                schalf[hb], kt_sb[:, j * 128:(j + 1) * 128],
                qt_sb[:, g * GQ:(g + 1) * GQ], start=True, stop=True),
                wait=wait, inc="PE")

        def emit_exp(g, j0, pe_sc, sc_exp):
            if estep == 2:
                buf = psA[:] if (j0 % 4 == 0) else psB[:]
                dst = ptil[:, j0 * GQ:(j0 + 2) * GQ]
            else:
                buf = schalf[j0 % 4]
                dst = pslot[j0]
            wait = [("PE", pe_sc[j0 + estep - 1]), ("DV", dv_mbias),
                    ("PE", max(ptil_slot_av[j0:j0 + estep]))]
            t_exp = p.op("scalar", lambda e, buf=buf, dst=dst: e.activation(
                dst, buf, AF.Exp, bias=mbias[:, 0:1], scale=C_ATTN),
                wait=wait, inc="SC")
            for jj in range(j0, j0 + estep):
                sc_exp[jj] = t_exp
                psbuf_last_rd[jj % 4] = t_exp

        pe_sc = {}
        sc_exp = {}
        mb_done = False
        for j in range(TPG):
            emit_score(0, j, pe_sc)
            if (j + 1) % estep == 0:
                if not mb_done:
                    emit_mb(h)
                    mb_done = True
                emit_exp(0, j + 1 - estep, pe_sc, sc_exp)
        for g in range(NG):
            chunks = (g + 1) * TPG
            gp_mask_j = {}
            for r in range(TPG):
                j = g * TPG + r
                if r > 0:
                    p.op("gpsimd", lambda e, j=j, r=r: e.memset(
                        ptil[:, j * GQ:j * GQ + r * 128], 0.0),
                        wait=[("SC", sc_exp[j])], inc="GP")
                gp_mask_j[j] = p.op("gpsimd", lambda e, j=j, r=r: e.tensor_tensor(
                    ptil[:, j * GQ + r * 128:j * GQ + (r + 1) * 128],
                    ptil[:, j * GQ + r * 128:j * GQ + (r + 1) * 128],
                    tri_sb[:], ALU.mult), wait=[("SC", sc_exp[j])], inc="GP")
            pe_den = 0
            for j in range(chunks):
                wait = [("SC", sc_exp[j])] if j < g * TPG else \
                    [("GP", gp_mask_j[j])]
                if j == 0:
                    wait.append(("SC", sc_rinv_prev))
                pe_den = p.op("tensor", lambda e, j=j, chunks=chunks: e.matmul(
                    psC[:, 512:512 + GQ], onem_sb[:], pslot[j],
                    start=(j == 0), stop=(j == chunks - 1)),
                    wait=wait, inc="PE" if j == chunks - 1 else None)
            p.drain("scalar")
            p.op("scalar", lambda e: e.activation(
                psC[:, 512:512 + GQ], psC[:, 512:512 + GQ], AF.Ln,
                bias=0.0, scale=1.0), wait=[("PE", pe_den)])
            p.drain("scalar")
            sc_rinv = p.op("scalar", lambda e, g=g: e.activation(
                rbc[g % 2], psC[:, 512:512 + GQ], AF.Exp, bias=0.0, scale=-1.0),
                wait=[("DV", rbc_war[g % 2])], inc="SC",
                lowp="softmax normalizer becomes a bf16 matmul operand")
            sc_rinv_prev = sc_rinv
            dv_norm = {}
            for j in range(chunks):
                wait = [("SC", sc_rinv)] if j == 0 else []
                if j == 0:
                    wait.append(("PE", pe_den))
                dv_norm[j] = p.op("vector", lambda e, j=j, g=g: e.tensor_tensor(
                    pslot[j], pslot[j], rbc[g % 2], ALU.mult),
                    wait=wait, inc="DV")
            rbc_war[g % 2] = dv_norm[chunks - 1]
            nchunks = (g + 2) * TPG
            if g + 1 < NG:
                pe_sc = {}
                for j in range(min(4, nchunks)):
                    emit_score(g + 1, j, pe_sc)
            elif h + 1 < H:
                emit_w_mms(h + 1, 0)
                emit_mh_sq(0, h + 1)
            pe_av = 0
            for j in range(chunks):
                wait = [("DV", dv_norm[j])]
                if j == 0:
                    wait += [("DV", dv_zevac.get(gidx - 1, 0)),
                             ("SC", max(sc_bw_evac_half))]
                pe_av = p.op("tensor", lambda e, j=j, chunks=chunks: e.matmul(
                    psE[:, 0:GQ], v_sb[:, j * 128:(j + 1) * 128], pslot[j],
                    start=(j == 0), stop=(j == chunks - 1)),
                    wait=wait, inc="PE")
                ptil_slot_av[j] = pe_av
            pe_av_all[h] = pe_av
            if g + 1 < NG:
                sc_exp = {}
                for j0 in range(0, min(4, nchunks), estep):
                    emit_exp(g + 1, j0, pe_sc, sc_exp)
                for j in range(min(4, nchunks), nchunks):
                    emit_score(g + 1, j, pe_sc)
                    if (j + 1) % estep == 0 and j + 1 > 4:
                        emit_exp(g + 1, j + 1 - estep, pe_sc, sc_exp)
            zpar = gidx % 2
            wait = [("PE", pe_av), (ZWs[zpar], zw_par_t[zpar])]
            dv_zevac[gidx] = p.op("vector", lambda e, zpar=zpar: e.tensor_copy(
                zstage[zpar], psE[:, 0:GQ]), wait=wait, inc="DV")
            zw_par_t[zpar] = p.op(
                "sync", lambda e, g=g, h=h, zpar=zpar: e.dma_start(
                    zs_h.ap()[g * TPG:(g + 1) * TPG, h].transpose([1, 0, 2]),
                    zstage[zpar].rearrange("p (t q) -> p t q", q=128)),
                wait=[("DV", dv_zevac[gidx])], inc=ZWs[zpar], incn=16)
            gidx += 1
            if g == NG - 1 and h + 1 < H:
                emit_mh_mms(0, h + 1)

    B_PE = p.cnt("PE")
    B_DV = p.cnt("DV")
    B_SC = p.cnt("SC")
    B_GP = p.cnt("GP")
    zw_tot = [p.cnt("ZW0"), p.cnt("ZW1")]
    wo_tot = p.cnt("WO")
    fw_tot = [p.cnt("FW0"), p.cnt("FW1")]
    ws_tot = [p.cnt(f"WS{i}") for i in range(4)]

    pe_proj = {}
    pe_ffn = {}
    sc_s1 = {}
    sc_s2 = {}
    gp_x2 = {}
    gp_x3 = {}
    dv_aggr2 = {}
    dv_ct = {}
    dv_ln3 = {}
    dv_ln3t = {}
    dv_x3ts = {}
    sc_r2 = {}
    sc_r3_prev = 0
    sc_r2_prev = 0
    sc_r4_prev = 0
    dv_ln3_prev = 0
    out_prev = 0
    xc_t = {}
    pe_tr3_d = {}
    ln3v = [ln3, v_sb[:, 0:DM]]
    projb = [psA[:, 0:PSW], psA[:, 512:512 + PSW],
             psB[:, 0:PSW], psB[:, 512:512 + PSW]][:NPS]
    ffnb = [psC[:, 0:PSW], psC[:, 512:512 + PSW],
            psD[:, 0:PSW], psE[:, 0:PSW]][:NPS]
    hn = max(NCH // 2, 1)

    def emit_front_pe(i):
        nonlocal sc_r2_prev
        par = i % 2
        wait = [("PE", pe_proj[i - 2])] if i - 2 in pe_proj else \
            [("PE", B0_PE), ("DV", B0_DV)]
        if i < 2:
            wait += [("ZW0", zw_tot[0]), ("ZW1", zw_tot[1])]
        if i == 1:
            wait.append(("PE", B_PE))
        zrd = (qt_sb, kt_sb)[par]
        zr_t = p.op("sync", lambda e, i=i, d=zrd: e.dma_start(
            d[:, 0:H * 128].rearrange("p (h q) -> p h q", q=128),
            zs_h.ap()[i].transpose([1, 0, 2])), wait=wait, inc=ZRs[par], incn=16)
        wait = [("DV", gp_x3[i - 2])] if i - 2 in gp_x3 else \
            [("PE", B0_PE), ("DV", B0_DV), ("SC", B0_SC), ("GP", B0_GP),
             (XAs[3], xa_tot[3])]
        xc_t[i] = p.op("sync", lambda e, i=i, par=par: e.dma_start(
            xin[par], x_h.ap()[i * 128:(i + 1) * 128, :]),
            wait=wait, inc=XCs[par], incn=16)
        for hh in range(H):
            for s in range(NPS):
                wait = []
                if hh == 0 and s == 0:
                    wait = [(ZRs[par], zr_t), ("WO", wo_tot),
                            ("SC", sc_s1.get(i - 1, B0_SC if i == 0 else 0)),
                            ("CONSTG", constg_t)]
                    if i == 0:
                        wait.append(("DV", B0_DV))
                p.op("tensor", lambda e, hh=hh, s=s, d=zrd: e.matmul(
                    projb[s], d[:, hh * 128:(hh + 1) * 128],
                    Bz[:, hh * DM + s * 512:hh * DM + s * 512 + PSW],
                    start=(hh == 0), stop=False), wait=wait)
        for s in range(NPS):
            tgt = p.op("tensor", lambda e, s=s: e.matmul(
                projb[s], oner_sb[:], bo_sb[0:1, s * 512:s * 512 + PSW],
                start=False, stop=True), inc="PE" if s == NPS - 1 else None)
        pe_proj[i] = tgt

    def emit_front_ln2(i):
        nonlocal sc_r2_prev
        par = i % 2
        wait = [("PE", pe_proj[i]), ("DV", dv_aggr2.get(i - 1, 0)),
                ("DV", gp_x2.get(i - 1, 0))]
        if i == 0:
            wait += [("DV", B0_DV), ("GP", B0_GP), ("ZW0", zw_tot[0]),
                     ("ZW1", zw_tot[1])]
        w1 = min(1024, DM)
        if NPS > 2:
            sc_a = p.op("scalar", lambda e, w1=w1: e.copy(
                s1[:, 0:w1], psA[:, 0:w1]), wait=wait, inc="SC")
            p.drain("scalar")
            sc_s1[i] = p.op("scalar", lambda e, w1=w1: e.copy(
                s1[:, w1:2 * w1], psB[:, 0:w1]), inc="SC")
        else:
            sc_a = sc_s1[i] = p.op("scalar", lambda e, w1=w1: e.copy(
                s1[:, 0:w1], psA[:, 0:w1]), wait=wait, inc="SC")
        for g in range(NST):
            hw = sc_a if (g * STW < w1) else sc_s1[i]
            w = [("SC", hw)]
            if g == 0:
                w.append(("SC", sc_r4_prev))
                if i == 0:
                    w.append(("GP", B_GP))
            p.op("vector", lambda e, g=g: e.bn_stats(
                st6[0][:, 6 * g:6 * g + 6], s1[:, STW * g:STW * (g + 1)]), wait=w)
        p.drain("vector")
        dv_aggr2[i] = p.op("vector", lambda e: e.bn_aggr(mv2[0][:], st6[0][:]),
                           inc="DV")
        sc_r2[i] = emit_rstd(0, dv_aggr2[i],
                             war_dv=max(dv_ct.get(i - 1, 0),
                                        gp_x2.get(i - 1, 0)))
        sc_r2_prev = sc_r2[i]
        p.drain("vector")
        dv_ct[i] = p.op("vector", lambda e, par=par: e.tensor_tensor(
            corr[:, par:par + 1], mv2[0][:, 0:1], rstd[0][:, 0:1], ALU.mult),
            wait=[("SC", sc_r2[i])], inc="DV")

    def emit_front_chain(i):
        nonlocal sc_r3_prev, dv_ln3_prev
        par = i % 2
        wait = [("SC", sc_r2[i]), (XCs[par], xc_t[i])]
        if i == 0:
            wait += [(FWs[0], fw_tot2[0]), (FWs[1], fw_tot2[1]),
                     ("DV", B_DV)]
        p.drain("vector")
        gp_x2[i] = p.op("vector", lambda e, par=par: e.scalar_tensor_tensor(
            xin[par], s1, rstd[0][:, 0:1], xin[par], ALU.mult, ALU.add),
            wait=wait, inc="DV")
        aggr3 = emit_ln_stats(xin[par], 1, [("DV", gp_x2[i])],
                              war_sc=sc_r3_prev)
        sc_r3 = emit_rstd(1, aggr3, war_dv=dv_ln3_prev)
        sc_r3_prev = sc_r3
        p.drain("vector")
        w3 = [("SC", sc_r3), ("PE", pe_tr3_d.get(i - 2, 0))]
        if i < 2:
            w3 += [("PE", B_PE), ("SC", B_SC)]
        dv_ln3[i] = p.op("vector", lambda e, par=par: e.tensor_scalar(
            ln3v[par], xin[par], mv2[1][:, 0:1], rstd[1][:, 0:1],
            ALU.subtract, ALU.mult), wait=w3, inc="DV")
        dv_ln3_prev = dv_ln3[i]

    def emit_back_tr_ffn(i):
        par = i % 2
        wait = [("DV", dv_ln3[i]),
                ("SC", sc_s2.get(i - 1, B_SC if i == 0 else 0))]
        if i == 0:
            wait.append(("DV", B_DV))
        pe_half = 0
        for c in range(NCH):
            w = wait if c == 0 else ()
            buf = psD_bf if c < hn else psE_bf
            cc = c % hn
            tgt = p.op("tensor", lambda e, c=c, buf=buf, cc=cc, par=par:
                       e.transpose(buf[:, cc * 128:(cc + 1) * 128],
                                   ln3v[par][:, c * 128:(c + 1) * 128],
                                   eye_sb[:]), wait=w,
                       inc="PE" if c in (hn - 1, NCH - 1) else None)
            if c == hn - 1:
                pe_half = tgt
        pe_tr3 = tgt
        pe_tr3_d[i] = pe_tr3
        dv_lo = p.op("vector", lambda e: e.tensor_copy(
            ln3t[:, 0:hn * 128], psD_bf[:, 0:hn * 128]),
            wait=[("PE", pe_half)], inc="DV")
        dv_ln3t[i] = p.op("vector", lambda e: e.tensor_copy(
            ln3t[:, hn * 128:NCH * 128], psE_bf[:, 0:(NCH - hn) * 128]),
            wait=[("PE", pe_tr3)], inc="DV") if NCH > hn else dv_lo
        for c in range(NCH):
            for s in range(NPS):
                wait = []
                if c == 0:
                    wait = [("DV", dv_ln3t[i] if s == NPS - 1 else dv_lo)]
                    if s == 0:
                        wait.append(
                            ("SC", sc_s2.get(i - 1, B_SC if i == 0 else 0)))
                        if i == 0:
                            wait.append(("DV", B_DV))
                p.op("tensor", lambda e, c=c, s=s: e.matmul(
                    ffnb[s], ln3t[:, c * 128:(c + 1) * 128],
                    bwt3[:, c, s * 512:s * 512 + PSW],
                    start=(c == 0), stop=False), wait=wait)
        for s in range(NPS):
            tgt = p.op("tensor", lambda e, s=s: e.matmul(
                ffnb[s], oner_sb[:], fb_sb[0:1, s * 512:s * 512 + PSW],
                start=False, stop=True), inc="PE" if s == NPS - 1 else None)
        pe_ffn[i] = tgt

    def emit_back_tail(i):
        nonlocal sc_r4_prev, out_prev
        par = i % 2
        wait = [("PE", pe_ffn[i]), ("DV", dv_ln3[i]),
                ("DV", dv_x3ts.get(i - 1, 0))]
        if i < 2:
            wait += [("PE", B_PE)] + [(f"WS{k}", ws_tot[k]) for k in range(4)]
        w1 = min(1024, DM)
        s2_copies = [(s2buf[:, 0:w1], psC[:, 0:w1])]
        if NPS > 2:
            s2_copies.append((s2buf[:, 1024:1536], psD[:]))
        if NPS > 3:
            s2_copies.append((s2buf[:, 1536:2048], psE[:]))
        last = None
        for ci, (d, srcp) in enumerate(s2_copies):
            if ci > 0:
                p.drain("scalar")
            last = p.op("scalar", lambda e, d=d, srcp=srcp: e.copy(d, srcp),
                        wait=(wait if ci == 0 else ()),
                        inc="SC" if ci == len(s2_copies) - 1 else None)
        sc_s2[i] = last
        aggr4 = emit_ln_stats(s2buf, 2, [("SC", sc_s2[i])], war_sc=sc_r4_prev)
        sc_r4 = emit_rstd(2, aggr4, war_dv=dv_x3ts.get(i - 1, 0))
        sc_r4_prev = sc_r4
        p.drain("vector")
        p.op("vector", lambda e: e.tensor_tensor(
            corr[:, 2:3], mv2[2][:, 0:1], rstd[2][:, 0:1], ALU.mult),
            wait=[("SC", sc_r4)])
        p.drain("vector")
        p.op("vector", lambda e, par=par: e.tensor_tensor(
            corr[:, 2:3], corr[:, 2:3], corr[:, par:par + 1], ALU.add))
        p.drain("vector")
        p.op("vector", lambda e: e.tensor_scalar(
            corr[:, 2:3], corr[:, 2:3], -1.0, None, ALU.mult))
        p.drain("vector")
        wait = [("OUT0", out_prev)]
        dv_x3ts[i] = p.op("vector", lambda e: e.tensor_scalar(
            x3buf, s2buf, rstd[2][:, 0:1], corr[:, 2:3],
            ALU.mult, ALU.add), wait=wait, inc="DV")
        p.drain("vector")
        gp_x3[i] = p.op("vector", lambda e, par=par: e.tensor_tensor(
            x3buf, x3buf, xin[par], ALU.add), inc="DV")
        out_prev = p.op("gpsimd", lambda e, i=i: e.dma_start(
            out_h.ap()[i * 128:(i + 1) * 128, :], x3buf),
            wait=[("DV", gp_x3[i])], inc="OUT0", incn=16)

    B0_PE = p.cnt("PE")
    B0_DV = p.cnt("DV")
    B0_SC = p.cnt("SC")
    B0_GP = p.cnt("GP")
    for t in range(NT):
        emit_bw_tile(t)
        if t == 4 and NT >= 12:
            emit_front_pe(0)
    emit_bw_evacs(NT - 1)

    B_PE = p.cnt("PE")
    B_DV = p.cnt("DV")
    B_SC = p.cnt("SC")
    B_GP = p.cnt("GP")
    fw_tot2 = [p.cnt("FW0"), p.cnt("FW1")]

    pre = 1 if NT >= 12 else 0
    for i in range(NT):
        if i >= pre:
            emit_front_pe(i)
        if i > 0:
            emit_back_tr_ffn(i - 1)
        emit_front_ln2(i)
        emit_front_chain(i)
        if i > 0:
            emit_back_tail(i - 1)
    emit_back_tr_ffn(NT - 1)
    emit_back_tail(NT - 1)

    p.op("sync", lambda e: e.nop(),
         wait=[("OUT0", p.cnt("OUT0"))])

    import contextlib
    with contextlib.ExitStack() as es:
        blk = es.enter_context(nc.Block())
        for eng in Plan.ENGS:
            getattr(blk, eng)(lambda e, eng=eng: [f(e) for f in p.plans[eng]])
    nc.finalize()
    return nc


_SEQ, _DM, _H, _B = 2048, 2048, 16, 8
_nc_cache = {}


def _get_nc():
    key = (_SEQ, _DM, _H)
    if key not in _nc_cache:
        _nc_cache[key] = build_program(*key)
    return _nc_cache[key]


def kernel(L0_attn_x, L0_attn_W_Q, L0_attn_W_K, L0_attn_W_V, L0_attn_W_O,
           L0_attn_b_Q, L0_attn_b_K, L0_attn_b_V, L0_attn_b_O,
           L0_ffn_weight, L0_ffn_bias):
    x = np.ascontiguousarray(np.asarray(L0_attn_x, dtype=np.float32))
    common = dict(
        wq=np.ascontiguousarray(np.asarray(L0_attn_W_Q, np.float32)),
        wk=np.ascontiguousarray(np.asarray(L0_attn_W_K, np.float32)),
        wv=np.ascontiguousarray(np.asarray(L0_attn_W_V, np.float32)),
        wo=np.ascontiguousarray(np.asarray(L0_attn_W_O, np.float32)),
        bqkvT=np.ascontiguousarray(np.concatenate(
            [np.asarray(L0_attn_b_Q, np.float32).T,
             np.asarray(L0_attn_b_K, np.float32).T,
             np.asarray(L0_attn_b_V, np.float32).T], axis=1)),
        bo=np.ascontiguousarray(np.asarray(L0_attn_b_O, np.float32)),
        fw=np.ascontiguousarray(np.asarray(L0_ffn_weight, np.float32)),
        fb=np.ascontiguousarray(np.asarray(L0_ffn_bias, np.float32)),
        eye=np.eye(128, dtype=BF),
        epsb=np.full((128, 1), 1e-5, np.float32),
        lncb=np.full((1, 1), math.log(1.0 / math.sqrt(128.0)), np.float32),
        tri=np.triu(np.ones((128, 128), np.float32)).astype(BF),
        onec=np.ones((128, 1), dtype=BF),
        oner=np.ones((1, 128), dtype=BF),
        onem=np.ones((128, 128), dtype=BF),
    )
    nc = _get_nc()
    in_maps = [dict(x=np.ascontiguousarray(x[b]), **common) for b in range(_B)]
    res = run_bass_kernel_spmd(nc, in_maps, core_ids=list(range(_B)))
    return np.stack([res.results[b]["out"] for b in range(_B)], axis=0)


# version 34 (final)
# speedup vs baseline: 1.0120x; 1.0060x over previous
import math
import numpy as np
import ml_dtypes
import concourse.bass as bass
import concourse.mybir as mybir
from concourse.bass_utils import run_bass_kernel_spmd

F32 = mybir.dt.float32
BF16 = mybir.dt.bfloat16
AF = mybir.ActivationFunctionType
ALU = mybir.AluOpType
AX = mybir.AxisListType
BF = ml_dtypes.bfloat16

LN_EPS = 1e-5


class Plan:
    ENGS = ("sync", "tensor", "vector", "scalar", "gpsimd")

    def __init__(self, nc):
        self.nc = nc
        self.plans = {e: [] for e in self.ENGS}
        self.sems = {}

    def sem(self, name):
        h = self.nc.alloc_semaphore(name)
        self.sems[name] = [h, 0]
        return name

    def cnt(self, name):
        return self.sems[name][1]

    def op(self, eng, fn, wait=(), inc=None, incn=1, lowp=None):
        tgt = None
        if inc is not None:
            self.sems[inc][1] += incn
            tgt = self.sems[inc][1]
        sems = self.sems
        nc = self.nc
        wait = tuple((s, v) for (s, v) in wait if v and v > 0)

        def thunk(e, fn=fn, wait=wait, inc=inc, incn=incn, lowp=lowp):
            for (s, v) in wait:
                e.wait_ge(sems[s][0], v)
            if lowp is not None:
                with nc.allow_low_precision(reason=lowp):
                    ins = fn(e)
            else:
                ins = fn(e)
            if inc is not None:
                ins.then_inc(sems[inc][0], incn)

        self.plans[eng].append(thunk)
        return tgt

    def drain(self, eng):
        self.plans[eng].append(lambda e: e.drain())


def build_program(SEQ, DM, H):
    assert SEQ == DM
    NCH = DM // 128
    NT = SEQ // 128
    SLW = min(512, SEQ)
    NSL = SEQ // SLW
    STW = min(512, DM)
    NST = DM // STW
    TPG = min(4, NT)
    NG = NT // TPG
    GQ = TPG * 128
    NPS = DM // 512 if DM >= 512 else 1
    PSW = min(512, DM)
    C_ATTN = 1.0 / math.sqrt(128.0)

    nc = bass.Bass("TRN2", target_bir_lowering=False)
    p = Plan(nc)

    x_h = nc.dram_tensor("x", [SEQ, DM], F32, kind="ExternalInput")
    wq_h = nc.dram_tensor("wq", [H, DM, 128], F32, kind="ExternalInput")
    wk_h = nc.dram_tensor("wk", [H, DM, 128], F32, kind="ExternalInput")
    wv_h = nc.dram_tensor("wv", [H, DM, 128], F32, kind="ExternalInput")
    wo_h = nc.dram_tensor("wo", [H, 128, DM], F32, kind="ExternalInput")
    bqkvT_h = nc.dram_tensor("bqkvT", [128, 3 * H], F32, kind="ExternalInput")
    bo_h = nc.dram_tensor("bo", [DM], F32, kind="ExternalInput")
    fw_h = nc.dram_tensor("fw", [DM, DM], F32, kind="ExternalInput")
    fb_h = nc.dram_tensor("fb", [DM], F32, kind="ExternalInput")
    eye_h = nc.dram_tensor("eye", [128, 128], BF16, kind="ExternalInput")
    tri_h = nc.dram_tensor("tri", [128, 128], BF16, kind="ExternalInput")
    onec_h = nc.dram_tensor("onec", [128, 1], BF16, kind="ExternalInput")
    oner_h = nc.dram_tensor("oner", [1, 128], BF16, kind="ExternalInput")
    onem_h = nc.dram_tensor("onem", [128, 128], BF16, kind="ExternalInput")
    epsb_h = nc.dram_tensor("epsb", [128, 1], F32, kind="ExternalInput")
    lncb_h = nc.dram_tensor("lncb", [1, 1], F32, kind="ExternalInput")
    out_h = nc.dram_tensor("out", [SEQ, DM], F32, kind="ExternalOutput")
    zs_h = nc.dram_tensor("zs", [NT, H, 128, 128], BF16)

    A = nc.alloc_sbuf_tensor("arena_a", [128, NCH * SEQ], BF16)
    Bz = nc.alloc_sbuf_tensor("arena_b", [128, H * DM], BF16)
    wbuf = nc.alloc_sbuf_tensor("wbuf", [128, 4 * NCH * 128], BF16)
    qt_sb = nc.alloc_sbuf_tensor("qt", [128, SEQ], BF16)
    kt_sb = nc.alloc_sbuf_tensor("kt", [128, SEQ], BF16)
    v_sb = nc.alloc_sbuf_tensor("v", [128, SEQ], BF16)
    ptil = nc.alloc_sbuf_tensor("ptil", [128, max(NT * GQ, 4 * DM)], BF16)
    C2 = nc.alloc_sbuf_tensor("c2", [128, max(6 * DM, 4 * GQ + SEQ + 4 * DM)], BF16)
    eye_sb = nc.alloc_sbuf_tensor("eye_sb", [128, 128], BF16)
    tri_sb = nc.alloc_sbuf_tensor("tri_sb", [128, 128], BF16)
    onec_sb = nc.alloc_sbuf_tensor("onec_sb", [128, 1], BF16)
    oner_sb = nc.alloc_sbuf_tensor("oner_sb", [1, 128], BF16)
    bqkvT_sb = nc.alloc_sbuf_tensor("bqkvT_sb", [128, 3 * H], F32)
    bo_sb = nc.alloc_sbuf_tensor("bo_sb", [1, DM], BF16)
    fb_sb = nc.alloc_sbuf_tensor("fb_sb", [1, DM], BF16)
    st6 = [nc.alloc_sbuf_tensor(f"st6_{i}", [128, 6 * NST], F32) for i in range(3)]
    mv2 = [nc.alloc_sbuf_tensor(f"mv2_{i}", [128, 2], F32) for i in range(3)]
    rstd = [nc.alloc_sbuf_tensor(f"rstd_{i}", [128, 1], F32) for i in range(3)]
    corr = nc.alloc_sbuf_tensor("corr", [128, 3], F32)
    onem_sb = nc.alloc_sbuf_tensor("onem_sb", [128, 128], BF16)
    mbias = nc.alloc_sbuf_tensor("mbias", [128, 1], F32)
    srow = nc.alloc_sbuf_tensor("srow", [1, 24], F32)
    nm_bf = nc.alloc_sbuf_tensor("nm_bf", [1, 1], BF16)
    asum = nc.alloc_sbuf_tensor("asum", [128, 1], F32)
    twoal = nc.alloc_sbuf_tensor("twoal", [128, 1], F32)
    negal = nc.alloc_sbuf_tensor("negal", [128, 1], F32)
    epsb_sb = nc.alloc_sbuf_tensor("epsb_sb", [128, 1], F32)
    lncb_sb = nc.alloc_sbuf_tensor("lncb_sb", [1, 1], F32)

    ln1t3 = A.rearrange("p (c s) -> p c s", c=NCH)
    bwt3 = ln1t3
    wslot = [wbuf[:, s * NCH * 128:(s + 1) * NCH * 128] for s in range(4)]
    x3v = wbuf.bitcast(F32)
    x3buf = x3v[:, 0:DM]
    s2buf = x3v[:, DM:2 * DM]
    pslot = [ptil[:, j * GQ:(j + 1) * GQ] for j in range(NT)]
    vstage = ptil[:, 0:SEQ]
    xinv = ptil.bitcast(F32)
    xin = [xinv[:, i * DM:(i + 1) * DM] for i in range(2)]
    c2f = C2.bitcast(F32)
    xa = [c2f[:, i * DM:(i + 1) * DM] for i in range(3)] + \
        [ptil.bitcast(F32)[:, SEQ:SEQ + DM]]
    zstage = [C2[:, i * GQ:(i + 1) * GQ] for i in range(2)]
    rbc = [C2[:, (2 + i) * GQ:(3 + i) * GQ] for i in range(2)]
    sqb = C2[:, 4 * GQ:4 * GQ + SEQ]
    fw_off = (4 * GQ + SEQ) // 2
    fwst = [c2f[:, fw_off + i * DM:fw_off + (i + 1) * DM] for i in range(2)]
    s1 = c2f[:, 0:DM]
    x2 = c2f[:, DM:2 * DM]
    ln3 = C2[:, 4 * DM:5 * DM]
    ln3t = C2[:, 5 * DM:5 * DM + NCH * 128]

    psA = nc.alloc_psum_tensor("psA", [128, 1024], F32)
    psB = nc.alloc_psum_tensor("psB", [128, 1024], F32)
    psC = nc.alloc_psum_tensor("psC", [128, 1024], F32)
    psD = nc.alloc_psum_tensor("psD", [128, 512], F32)
    psE = nc.alloc_psum_tensor("psE", [128, 512], F32)
    psA_bf = psA.bitcast(BF16)
    psB_bf = psB.bitcast(BF16)
    psC_bf = psC.bitcast(BF16)
    psD_bf = psD.bitcast(BF16)
    psE_bf = psE.bitcast(BF16)
    schalf = [psA[:, 0:GQ], psA[:, 512:512 + GQ],
              psB[:, 0:GQ], psB[:, 512:512 + GQ]]

    for s in ("CONST", "CONSTG", "ZW0", "ZW1", "OUT0", "OUT1", "WO", "PE",
              "DV", "SC", "GP"):
        p.sem(s)
    XAs = [p.sem(f"XA{i}") for i in range(4)]
    XCs = [p.sem(f"XC{i}") for i in range(2)]
    ZRs = [p.sem(f"ZR{i}") for i in range(2)]
    FWs = [p.sem(f"FW{i}") for i in range(2)]
    WSs = [p.sem(f"WS{i}") for i in range(4)]
    ZWs = ("ZW0", "ZW1")
    OUTs = ("OUT0", "OUT1")

    for dst, src in ((eye_sb, eye_h), (tri_sb, tri_h), (onec_sb, onec_h),
                     (oner_sb, oner_h), (onem_sb, onem_h), (bqkvT_sb, bqkvT_h),
                     (epsb_sb, epsb_h), (lncb_sb, lncb_h)):
        p.op("sync", lambda e, d=dst, s=src: e.dma_start(d[:], s.ap()),
             inc="CONST", incn=16)
    const_t = p.cnt("CONST")
    p.op("gpsimd", lambda e: e.dma_start(
        bo_sb[:], bo_h.ap().rearrange("(o d) -> o d", o=1)), inc="CONSTG", incn=16)
    p.op("gpsimd", lambda e: e.dma_start(
        fb_sb[:], fb_h.ap().rearrange("(o d) -> o d", o=1)), inc="CONSTG", incn=16)
    constg_t = p.cnt("CONSTG")

    w_handles = (wq_h, wk_h, wv_h)
    ws_t = {}
    pe_w_done = {}

    def emit_w_load(h, wi):
        slot = (3 * h + wi) % 4
        wait = []
        m = 3 * h + wi - 4
        if m >= 0:
            hu, iu = divmod(m, 3)
            wait.append(("PE", pe_w_done[(hu, iu)]))
        src = w_handles[wi].ap()[h].rearrange("(c q) e -> q c e", q=128)
        ws_t[(h, wi)] = p.op(
            "gpsimd", lambda e, d=wslot[slot], s=src: e.dma_start(d, s),
            wait=wait, inc=WSs[slot], incn=16)

    for wi in range(3):
        emit_w_load(0, wi)

    def emit_ln_stats(src_ap, par, wait, war_sc=0):
        for g in range(NST):
            w = list(wait) if g == 0 else []
            if g == 0 and war_sc:
                w.append(("SC", war_sc))
            p.op("vector", lambda e, g=g, a=src_ap, pr=par: e.bn_stats(
                st6[pr][:, 6 * g:6 * g + 6], a[:, STW * g:STW * (g + 1)]), wait=w)
        p.drain("vector")
        return p.op("vector", lambda e, pr=par: e.bn_aggr(mv2[pr][:], st6[pr][:]),
                    inc="DV")

    def emit_rstd(par, wait_dv, war_dv=0, extra=()):
        w = [("DV", wait_dv)] + [t for t in extra if t[1]]
        if war_dv:
            w.append(("DV", war_dv))
        w.append(("CONST", const_t))
        p.op("scalar", lambda e, pr=par: e.activation(
            rstd[pr][:], mv2[pr][:, 1:2], AF.Ln, bias=epsb_sb[:, 0:1],
            scale=1.0), wait=w)
        p.drain("scalar")
        return p.op("scalar", lambda e, pr=par: e.activation(
            rstd[pr][:], rstd[pr][:], AF.Exp, bias=0.0, scale=-0.5), inc="SC")

    dv_apply = {}
    dv_evacA = {}
    pe_trA = {}
    sc_rstdA = {}
    ln1bf = [v_sb, ptil[:, SEQ:2 * SEQ]]
    xa_tot = [0, 0, 0, 0]

    def emit_A_head(t):
        slot = t % 4
        par = t % 3
        wait = [("DV", dv_apply[t - 4])] if t - 4 in dv_apply else []
        xt = p.op(("sync", "gpsimd")[slot % 2], lambda e, sl=slot, t=t:
                  e.dma_start(xa[sl], x_h.ap()[t * 128:(t + 1) * 128, :]),
                  wait=wait, inc=XAs[slot], incn=16)
        xa_tot[slot] = xt
        aggr = emit_ln_stats(xa[slot], par, [(XAs[slot], xt)],
                             war_sc=sc_rstdA.get(t - 3, 0))
        sc_rstdA[t] = emit_rstd(par, aggr, war_dv=dv_apply.get(t - 3, 0))

    def emit_A_tail(t):
        par = t % 3
        slot = t % 4
        bpar = t % 2
        wait = [("SC", sc_rstdA[t])]
        if t - 2 in pe_trA:
            wait.append(("PE", pe_trA[t - 2]))
        p.drain("vector")
        hn2 = max(NCH // 2, 1)
        hw = hn2 * 128
        dv_apply_h = [0, 0]
        dv_apply_h[0] = p.op("vector", lambda e, pr=par, sl=slot, bp=bpar,
                             hw=hw: e.tensor_scalar(
            ln1bf[bp][:, 0:hw], xa[sl][:, 0:hw], mv2[pr][:, 0:1],
            rstd[pr][:, 0:1], ALU.subtract, ALU.mult), wait=wait, inc="DV")
        dv_apply_h[1] = dv_apply[t] = p.op(
            "vector", lambda e, pr=par, sl=slot, bp=bpar, hw=hw:
            e.tensor_scalar(
                ln1bf[bp][:, hw:NCH * 128], xa[sl][:, hw:NCH * 128],
                mv2[pr][:, 0:1], rstd[pr][:, 0:1],
                ALU.subtract, ALU.mult), inc="DV") \
            if NCH > 1 else dv_apply_h[0]
        if NCH == 1:
            dv_apply[t] = dv_apply_h[0]
        last_sc = 0
        for half in range(2 if NCH > 1 else 1):
            b = (psD_bf, psE_bf)[half]
            c0 = half * hn2
            c1 = NCH if half else hn2
            wait = [("DV", dv_apply_h[half]), ("CONST", const_t),
                    ("SC", sc_evacA.get((t - 1, half), 0))]
            for c in range(c0, c1):
                w = wait if c == c0 else ()
                tgt = p.op("tensor", lambda e, c=c, b=b, c0=c0, bp=bpar:
                           e.transpose(b[:, (c - c0) * 128:(c - c0 + 1) * 128],
                                       ln1bf[bp][:, c * 128:(c + 1) * 128],
                                       eye_sb[:]), wait=w,
                           inc="PE" if c == c1 - 1 else None)
            pe_trA[t] = tgt
            p.drain("scalar")
            last_sc = p.op("scalar", lambda e, t=t, b=b, c0=c0, c1=c1: e.copy(
                ln1t3[:, c0:c1, t * 128:(t + 1) * 128],
                b[:, 0:(c1 - c0) * 128].rearrange("p (c q) -> p c q", c=c1 - c0)),
                wait=[("PE", pe_trA[t])], inc="SC")
            sc_evacA[(t, half)] = last_sc
        sc_evacA[t] = last_sc
        dv_evacA[t] = 0

    qkv_dst = (qt_sb[:], kt_sb[:], vstage)
    psbuf_last_rd = [0, 0, 0, 0]
    ptil_slot_av = [0] * NT
    pe_av_all = {}
    dv_zevac = {}
    zw_par_t = [0, 0]
    dv_bank_free = 0
    pe_norm_last = 0
    dv_vch = {}
    dv_vch_lo = {}
    gidx = 0
    sc_rinv_prev = 0
    dv_mbias_prev = 0
    pe_mb_prev = 0
    dv_red_last = 0
    dv_mx_d = {1: 0}
    dv_sq_d = {}
    dv_red_d = {}
    dv_nm_cur = 0
    dv_mbias = 0
    sc_lns = 0
    rbc_war = [0, 0]
    sc_after_masks = 0

    SET0 = None
    dv_qkv_evac_d = {}
    dv_qkv_evac_s = {}
    pe_w_done_d = {}

    sc_bw_evac_half = [0, 0]
    sc_sign_d = {}
    dv_fw_free = {}
    pe_bw_last = 0
    pe_bw_par = [0, 0]
    pe_bw_tr = {}

    def emit_w_mms(h, wi, only_slice=None, defer_evac=False):
        nonlocal dv_bank_free
        bankset = [[psD[:, 0:SLW], psE[:, 0:SLW], psC[:, 0:SLW],
                    psC[:, 512:512 + SLW]],
                   [psA[:, 0:SLW], psA[:, 512:512 + SLW], psB[:, 0:SLW],
                    psB[:, 512:512 + SLW]]]
        sidx = 0 if wi == 1 else 1
        banks = bankset[sidx]
        slot = (3 * h + wi) % 4
        dst = (qt_sb[:], kt_sb[:], vstage)[wi]
        tpsl = NT // NSL
        pe_mm = 0
        deferred = []
        if only_slice is not None:
            slices = [(only_slice, 1)]
        else:
            slices = [(sq0, min(4, NSL - sq0)) for sq0 in range(0, NSL, 4)]
        for sq0, nq in slices:
            for c in range(NCH):
                wait = []
                if c == 0:
                    wait = [(WSs[slot], ws_t[(h, wi)])]
                    if only_slice is not None:
                        wait.append(("DV", dv_qkv_evac_d.get((h, wi), 0)))
                    if sidx == 0:
                        wait += [("DV", dv_zevac.get(gidx - 1, 0)),
                                 ("DV", dv_mbias_prev),
                                 ("SC", sc_rinv_prev),
                                 ("SC", max(sc_bw_evac_half)),
                                 ("DV", dv_qkv_evac_d.get((h, 0), dv_bank_free))]
                    else:
                        wait += [("SC", max(psbuf_last_rd)),
                                 ("DV", dv_vch.get(h - 1, 0)),
                                 ("DV", dv_red_last)]
                        if wi == 2:
                            wait.append(("DV", dv_qkv_evac_d.get((h, 0), 0)))
                for k in range(nq):
                    s = sq0 + k
                    w = list(wait) if k == 0 else []
                    if c == 0:
                        w.append(("SC", sc_evacA[min((s + 1) * tpsl, NT) - 1]))
                    w = w or ()
                    bidx = (2 + s % 2) if (only_slice is not None and wi == 1) \
                        else (s % 4)
                    tgt = p.op(
                        "tensor", lambda e, c=c, s=s, sl=slot, banks=banks,
                        bidx=bidx:
                        e.matmul(banks[bidx],
                                 wslot[sl][:, c * 128:(c + 1) * 128],
                                 ln1t3[:, c, s * SLW:(s + 1) * SLW],
                                 start=(c == 0), stop=(c == NCH - 1)),
                        wait=w,
                        inc="PE" if (c == NCH - 1 and k == nq - 1) else None)
            pe_mm = tgt

            def do_evacs(sq0=sq0, nq=nq, pe_mm=pe_mm, banks=banks):
                nonlocal dv_bank_free
                for k in range(nq):
                    s = sq0 + k
                    bidx = (2 + s % 2) if (only_slice is not None and
                                           wi == 1) else (s % 4)
                    dv_bank_free = p.op(
                        "vector", lambda e, s=s, d=dst, wi=wi, h=h,
                        banks=banks, bidx=bidx:
                        e.tensor_scalar(d[:, s * SLW:(s + 1) * SLW],
                                        banks[bidx],
                                        bqkvT_sb[:, wi * H + h:
                                                 wi * H + h + 1],
                                        None, ALU.add),
                        wait=[("PE", pe_mm)], inc="DV")
                    dv_qkv_evac_s[(h, wi, s)] = dv_bank_free
            if defer_evac:
                deferred.append(do_evacs)
            else:
                do_evacs()
        if (h, wi) in pe_w_done_d:
            pe_mm = max(pe_mm, pe_w_done_d[(h, wi)])
        pe_w_done_d[(h, wi)] = pe_mm
        pe_w_done[(h, wi)] = pe_mm

        def finish():
            nonlocal dv_bank_free
            for f in deferred:
                f()
            cur = dv_qkv_evac_d.get((h, wi), 0)
            dv_qkv_evac_d[(h, wi)] = max(cur, dv_bank_free)
        if defer_evac:
            return finish
        finish()

    sc_evacA = {}
    wA_flush = {}
    for t in range(NT + 2):
        if t < NT:
            emit_A_head(t)
        if t - 2 >= 0:
            emit_A_tail(t - 2)
            tp = t - 2
            for f in wA_flush.pop(tp, ()):
                f()
            if (tp + 1) % (NT // NSL) == 0 and H > 0:
                s = (tp + 1) // (NT // NSL) - 1
                flq = emit_w_mms(0, 0, only_slice=s, defer_evac=True)
                flk = emit_w_mms(0, 1, only_slice=s, defer_evac=True)
                flq()
                flv = emit_w_mms(0, 2, only_slice=s, defer_evac=True)
                wA_flush[tp + 3] = [flk, flv]
    for fl in sorted(wA_flush):
        for f in wA_flush[fl]:
            f()

    sc_ln1t_all = sc_evacA[NT - 1]

    def emit_bw_evacs(t):
        hn2 = max(NCH // 2, 1)
        for half in range(2 if NCH > 1 else 1):
            b = (psD_bf, psE_bf)[half]
            c0 = half * hn2
            c1 = NCH if half else hn2
            sc_bw_evac_half[half] = p.op(
                "scalar", lambda e, t=t, b=b, c0=c0, c1=c1: e.copy(
                    bwt3[:, c0:c1, t * 128:(t + 1) * 128],
                    b[:, 0:(c1 - c0) * 128].rearrange("p (c q) -> p c q",
                                                      c=c1 - c0)),
                wait=[("PE", pe_bw_tr[(t, half)])], inc="SC")

    def emit_bw_tile(t):
        nonlocal pe_bw_last
        fpar = t % 2
        stg = (v_sb, kt_sb)[fpar]
        wait = [("SC", sc_ln1t_all),
                (XAs[1], xa_tot[1]), (XAs[2], xa_tot[2])] if t < 2 else []
        if t - 2 in dv_fw_free:
            wait = [("DV", dv_fw_free[t - 2]), ("SC", sc_sign_d[t - 2])]
        fw_t = p.op("gpsimd", lambda e, t=t, fpar=fpar: e.dma_start(
            fwst[fpar], fw_h.ap()[t * 128:(t + 1) * 128, :]),
            wait=wait, inc=FWs[fpar], incn=16)
        dv_as = p.op("vector", lambda e, fpar=fpar: e.reduce_sum(
            asum[:], fwst[fpar], AX.X, apply_absolute_value=True),
            wait=[(FWs[fpar], fw_t)], inc="DV")
        p.drain("vector")
        dv_al = p.op("vector", lambda e: e.tensor_scalar(
            twoal[:], asum[:], 1.0 / DM, None, ALU.mult), inc="DV")
        dv_fw_free[t] = dv_as
        w2 = [(FWs[fpar], fw_t), ("PE", pe_bw_par[fpar])]
        if t < 2:
            w2.append(("PE", pe_av_all[H - 1]))
        sc_sign_d[t] = p.op("scalar", lambda e, fpar=fpar, stg=stg:
                            e.activation(stg[:], fwst[fpar], AF.Sign,
                                         bias=0.0, scale=1.0),
                            wait=w2, inc="SC",
                            lowp="binarized weight is a bf16 matmul operand")
        if t >= 1:
            emit_bw_evacs(t - 1)
        dv_bw = p.op("vector", lambda e, stg=stg: e.tensor_scalar(
            stg[:], stg[:], twoal[:, 0:1], None, ALU.mult),
            wait=[("SC", sc_sign_d[t]), ("DV", dv_al)], inc="DV",
            lowp="binarized weight is a bf16 matmul operand")
        hn2 = max(NCH // 2, 1)
        for half in range(2 if NCH > 1 else 1):
            b = (psD_bf, psE_bf)[half]
            c0 = half * hn2
            c1 = NCH if half else hn2
            wait = [("DV", dv_bw), ("SC", sc_bw_evac_half[half]),
                    ("DV", dv_zevac.get(gidx - 1, 0))]
            for c in range(c0, c1):
                w3 = wait if c == c0 else ()
                tgt = p.op("tensor", lambda e, c=c, b=b, c0=c0, stg=stg:
                           e.transpose(b[:, (c - c0) * 128:(c - c0 + 1) * 128],
                                       stg[:, c * 128:(c + 1) * 128],
                                       eye_sb[:]), wait=w3,
                           inc="PE" if c == c1 - 1 else None)
            pe_bw_tr[(t, half)] = tgt
            pe_bw_last = tgt
        pe_bw_par[fpar] = pe_bw_last

    def emit_mh_sq(which, h):
        src = (qt_sb, kt_sb)[which]
        for s in range(NSL):
            w0 = [("DV", dv_qkv_evac_s.get((h, which, s),
                                           dv_qkv_evac_d[(h, which)]))]
            if s == 0:
                w0.append(("PE", pe_norm_last))
                if which == 0:
                    w0 += [("PE", pe_mb_prev), ("PE", pe_bw_last)]
            dv_sq_d[(which, s)] = p.op(
                "vector", lambda e, s=s, src=src: e.tensor_mul(
                    sqb[:, s * SLW:(s + 1) * SLW],
                    src[:, s * SLW:(s + 1) * SLW],
                    src[:, s * SLW:(s + 1) * SLW]), wait=w0, inc="DV")

    def emit_mh_mms(which, h):
        nonlocal pe_norm_last, dv_red_last, sc_lns
        for s in range(NSL):
            hb = s % 4
            wait = [("DV", dv_sq_d[(which, s)]), ("CONST", const_t),
                    ("SC", psbuf_last_rd[hb])]
            if which == 1:
                wait.append(("DV", dv_red_d[(0, s)]))
                wait.append(("DV", dv_qkv_evac_s.get(
                    (h, 2, s), dv_qkv_evac_d[(h, 2)])))
            elif s == 0 and h == 0:
                wait.append(("DV", dv_qkv_evac_d[(h, 2)]))
            pe_norm_last = p.op("tensor", lambda e, s=s, hb=hb: e.matmul(
                [psA[:, 0:SLW], psA[:, 512:512 + SLW], psB[:, 0:SLW],
                 psB[:, 512:512 + SLW]][hb], onem_sb[:],
                sqb[:, s * SLW:(s + 1) * SLW], start=True, stop=True),
                wait=wait, inc="PE")
            dv_red_d[(which, s)] = p.op(
                "vector", lambda e, s=s, hb=hb, wh=which: e.reduce_max(
                    srow[0:1, wh * 8 + s:wh * 8 + s + 1],
                    [psA[0:1, 0:SLW], psA[0:1, 512:512 + SLW],
                     psB[0:1, 0:SLW], psB[0:1, 512:512 + SLW]][hb], AX.X),
                wait=[("PE", pe_norm_last)], inc="DV")
        p.drain("vector")
        dv_mx_d[which] = p.op("vector", lambda e, wh=which: e.reduce_max(
            srow[0:1, 16 + wh:17 + wh], srow[0:1, wh * 8:wh * 8 + NSL],
            AX.X), inc="DV")
        dv_red_last = dv_mx_d[which]
        p.drain("scalar")
        t_ln = p.op("scalar", lambda e, wh=which: e.activation(
            srow[0:1, 18 + wh:19 + wh], srow[0:1, 16 + wh:17 + wh], AF.Ln,
            bias=0.0, scale=1.0), wait=[("DV", dv_mx_d[which])],
            inc="SC" if which == 1 else None)
        if which == 1:
            sc_lns = t_ln

    def emit_mh_tail(h):
        nonlocal dv_nm_cur
        p.drain("vector")
        dv_add = p.op("vector", lambda e: e.tensor_tensor(
            srow[0:1, 20:21], srow[0:1, 18:19], srow[0:1, 19:20], ALU.add),
            wait=[("SC", sc_lns)], inc="DV")
        sc_M = p.op("scalar", lambda e: e.activation(
            srow[0:1, 21:22], srow[0:1, 20:21], AF.Exp,
            bias=lncb_sb[0:1, 0:1], scale=0.5),
            wait=[("DV", dv_add)], inc="SC")
        p.drain("vector")
        dv_nm_cur = p.op("vector", lambda e: e.tensor_scalar(
            nm_bf[:], srow[0:1, 21:22], -1.0, None, ALU.mult),
            wait=[("SC", sc_M)], inc="DV", lowp="softmax shift operand; "
            "softmax is shift-invariant so the value need not be exact")

    def emit_mb(h):
        nonlocal pe_mb_prev, dv_mbias, dv_mbias_prev
        pe_mb = p.op("tensor", lambda e: e.matmul(
            psC[:, 0:1], oner_sb[:], nm_bf[:], start=True, stop=True),
            wait=[("DV", dv_nm_cur), ("DV", dv_qkv_evac_d[(h, 1)]),
                  ("SC", sc_rinv_prev)], inc="PE")
        pe_mb_prev = pe_mb
        dv_mbias = p.op("vector", lambda e: e.tensor_copy(
            mbias[:], psC[:, 0:1]), wait=[("PE", pe_mb)], inc="DV")
        dv_mbias_prev = dv_mbias

    for h in range(H):
        if h > 0:
            emit_w_mms(h, 1)
            emit_mh_sq(1, h)
            emit_w_mms(h, 2)
        dv_qkv_evac = [dv_qkv_evac_d[(h, wi)] for wi in range(3)]
        dv_qkv_all = max(dv_qkv_evac)
        if h + 1 < H:
            for wi in range(3):
                emit_w_load(h + 1, wi)
        wo_start = max(0, H - 4)
        if h >= wo_start:
            win = H - wo_start
            per = (H + win - 1) // win
            hh0 = (h - wo_start) * per
            for hh in range(hh0, min(hh0 + per, H)):
                p.op("gpsimd", lambda e, hh=hh: e.dma_start(
                    Bz[:, hh * DM:(hh + 1) * DM], wo_h.ap()[hh]),
                    inc="WO", incn=16)

        if h == 0:
            emit_mh_sq(0, 0)
            emit_mh_mms(0, 0)
            emit_mh_sq(1, 0)
        emit_mh_mms(1, h)
        emit_mh_tail(h)

        wait = [("DV", dv_qkv_all), ("DV", dv_mx_d[1])]
        nh = NT // 2
        pe_vtr_half = 0
        for t in range(NT):
            w = wait if t == 0 else ()
            tgt = p.op("tensor", lambda e, t=t: e.transpose(
                psA_bf[:, t * 128:(t + 1) * 128], vstage[:, t * 128:(t + 1) * 128],
                eye_sb[:]), wait=w,
                inc="PE" if t in (nh - 1, NT - 1) else None)
            if t == nh - 1:
                pe_vtr_half = tgt
        pe_vtr = tgt
        for j in range(max(SEQ // GQ, 1)):
            ptil_slot_av[j] = max(ptil_slot_av[j], pe_vtr)
        dv_vch_lo[h] = p.op("vector", lambda e, nh=nh: e.tensor_copy(
            v_sb[:, 0:nh * 128], psA_bf[:, 0:nh * 128]),
            wait=[("PE", pe_vtr_half)], inc="DV")
        dv_vch[h] = p.op("vector", lambda e, nh=nh: e.tensor_copy(
            v_sb[:, nh * 128:NT * 128], psA_bf[:, nh * 128:NT * 128]),
            wait=[("PE", pe_vtr)], inc="DV")

        estep = 2 if GQ == 512 else 1

        def emit_score(g, j, pe_sc):
            hb = j % 4
            wait = [("SC", psbuf_last_rd[hb])]
            split_ok = NT * 128 == 2048
            if j == 0:
                wait.append(("DV", dv_qkv_all))
                wait.append(("DV", dv_vch_lo[h] if (g == 0 and split_ok)
                             else dv_vch[h]))
                if g == 0:
                    wait.append(("DV", dv_mx_d[1]))
            elif g == 0 and j == 1 and split_ok:
                wait.append(("DV", dv_vch[h]))
            pe_sc[j] = p.op("tensor", lambda e, j=j, g=g, hb=hb: e.matmul(
                schalf[hb], kt_sb[:, j * 128:(j + 1) * 128],
                qt_sb[:, g * GQ:(g + 1) * GQ], start=True, stop=True),
                wait=wait, inc="PE")

        def emit_exp(g, j0, pe_sc, sc_exp):
            if estep == 2:
                buf = psA[:] if (j0 % 4 == 0) else psB[:]
                dst = ptil[:, j0 * GQ:(j0 + 2) * GQ]
            else:
                buf = schalf[j0 % 4]
                dst = pslot[j0]
            wait = [("PE", pe_sc[j0 + estep - 1]), ("DV", dv_mbias),
                    ("PE", max(ptil_slot_av[j0:j0 + estep]))]
            t_exp = p.op("scalar", lambda e, buf=buf, dst=dst: e.activation(
                dst, buf, AF.Exp, bias=mbias[:, 0:1], scale=C_ATTN),
                wait=wait, inc="SC")
            for jj in range(j0, j0 + estep):
                sc_exp[jj] = t_exp
                psbuf_last_rd[jj % 4] = t_exp

        pe_sc = {}
        sc_exp = {}
        mb_done = False
        for j in range(TPG):
            emit_score(0, j, pe_sc)
            if (j + 1) % estep == 0:
                if not mb_done:
                    emit_mb(h)
                    mb_done = True
                emit_exp(0, j + 1 - estep, pe_sc, sc_exp)
        for g in range(NG):
            chunks = (g + 1) * TPG
            gp_mask_j = {}
            for r in range(TPG):
                j = g * TPG + r
                if r > 0:
                    p.op("gpsimd", lambda e, j=j, r=r: e.memset(
                        ptil[:, j * GQ:j * GQ + r * 128], 0.0),
                        wait=[("SC", sc_exp[j])], inc="GP")
                gp_mask_j[j] = p.op("gpsimd", lambda e, j=j, r=r: e.tensor_tensor(
                    ptil[:, j * GQ + r * 128:j * GQ + (r + 1) * 128],
                    ptil[:, j * GQ + r * 128:j * GQ + (r + 1) * 128],
                    tri_sb[:], ALU.mult), wait=[("SC", sc_exp[j])], inc="GP")
            pe_den = 0
            for j in range(chunks):
                wait = [("SC", sc_exp[j])] if j < g * TPG else \
                    [("GP", gp_mask_j[j])]
                if j == 0:
                    wait.append(("SC", sc_rinv_prev))
                pe_den = p.op("tensor", lambda e, j=j, chunks=chunks: e.matmul(
                    psC[:, 512:512 + GQ], onem_sb[:], pslot[j],
                    start=(j == 0), stop=(j == chunks - 1)),
                    wait=wait, inc="PE" if j == chunks - 1 else None)
            p.drain("scalar")
            p.op("scalar", lambda e: e.activation(
                psC[:, 512:512 + GQ], psC[:, 512:512 + GQ], AF.Ln,
                bias=0.0, scale=1.0), wait=[("PE", pe_den)])
            p.drain("scalar")
            sc_rinv = p.op("scalar", lambda e, g=g: e.activation(
                rbc[g % 2], psC[:, 512:512 + GQ], AF.Exp, bias=0.0, scale=-1.0),
                wait=[("DV", rbc_war[g % 2])], inc="SC",
                lowp="softmax normalizer becomes a bf16 matmul operand")
            sc_rinv_prev = sc_rinv
            dv_norm = {}
            for j in range(chunks):
                wait = [("SC", sc_rinv)] if j == 0 else []
                if j == 0:
                    wait.append(("PE", pe_den))
                dv_norm[j] = p.op("vector", lambda e, j=j, g=g: e.tensor_tensor(
                    pslot[j], pslot[j], rbc[g % 2], ALU.mult),
                    wait=wait, inc="DV")
            rbc_war[g % 2] = dv_norm[chunks - 1]
            nchunks = (g + 2) * TPG
            if g + 1 < NG:
                pe_sc = {}
                for j in range(min(4, nchunks)):
                    emit_score(g + 1, j, pe_sc)
            elif h + 1 < H:
                emit_w_mms(h + 1, 0)
                emit_mh_sq(0, h + 1)
            pe_av = 0
            for j in range(chunks):
                wait = [("DV", dv_norm[j])]
                if j == 0:
                    wait += [("DV", dv_zevac.get(gidx - 1, 0)),
                             ("SC", max(sc_bw_evac_half))]
                pe_av = p.op("tensor", lambda e, j=j, chunks=chunks: e.matmul(
                    psE[:, 0:GQ], v_sb[:, j * 128:(j + 1) * 128], pslot[j],
                    start=(j == 0), stop=(j == chunks - 1)),
                    wait=wait, inc="PE")
                ptil_slot_av[j] = pe_av
            pe_av_all[h] = pe_av
            if g + 1 < NG:
                sc_exp = {}
                for j0 in range(0, min(4, nchunks), estep):
                    emit_exp(g + 1, j0, pe_sc, sc_exp)
                for j in range(min(4, nchunks), nchunks):
                    emit_score(g + 1, j, pe_sc)
                    if (j + 1) % estep == 0 and j + 1 > 4:
                        emit_exp(g + 1, j + 1 - estep, pe_sc, sc_exp)
            zpar = gidx % 2
            wait = [("PE", pe_av), (ZWs[zpar], zw_par_t[zpar])]
            dv_zevac[gidx] = p.op("vector", lambda e, zpar=zpar: e.tensor_copy(
                zstage[zpar], psE[:, 0:GQ]), wait=wait, inc="DV")
            zw_par_t[zpar] = p.op(
                "sync", lambda e, g=g, h=h, zpar=zpar: e.dma_start(
                    zs_h.ap()[g * TPG:(g + 1) * TPG, h].transpose([1, 0, 2]),
                    zstage[zpar].rearrange("p (t q) -> p t q", q=128)),
                wait=[("DV", dv_zevac[gidx])], inc=ZWs[zpar], incn=16)
            gidx += 1
            if g == NG - 1 and h + 1 < H:
                emit_mh_mms(0, h + 1)

    B_PE = p.cnt("PE")
    B_DV = p.cnt("DV")
    B_SC = p.cnt("SC")
    B_GP = p.cnt("GP")
    zw_tot = [p.cnt("ZW0"), p.cnt("ZW1")]
    wo_tot = p.cnt("WO")
    fw_tot = [p.cnt("FW0"), p.cnt("FW1")]
    ws_tot = [p.cnt(f"WS{i}") for i in range(4)]

    pe_proj = {}
    pe_ffn = {}
    sc_s1 = {}
    sc_s2 = {}
    gp_x2 = {}
    gp_x3 = {}
    dv_aggr2 = {}
    dv_ct = {}
    dv_ln3 = {}
    dv_ln3t = {}
    dv_x3ts = {}
    sc_r2 = {}
    sc_r3_prev = 0
    sc_r2_prev = 0
    sc_r4_prev = 0
    dv_ln3_prev = 0
    out_prev = 0
    xc_t = {}
    pe_tr3_d = {}
    ln3v = [ln3, v_sb[:, 0:DM]]
    projb = [psA[:, 0:PSW], psA[:, 512:512 + PSW],
             psB[:, 0:PSW], psB[:, 512:512 + PSW]][:NPS]
    ffnb = [psC[:, 0:PSW], psC[:, 512:512 + PSW],
            psD[:, 0:PSW], psE[:, 0:PSW]][:NPS]
    hn = max(NCH // 2, 1)

    def emit_front_pe(i):
        nonlocal sc_r2_prev
        par = i % 2
        wait = [("PE", pe_proj[i - 2])] if i - 2 in pe_proj else \
            [("PE", B0_PE), ("DV", B0_DV)]
        if i < 2:
            wait += [("ZW0", zw_tot[0]), ("ZW1", zw_tot[1])]
        if i == 1:
            wait.append(("PE", B_PE))
        zrd = (qt_sb, kt_sb)[par]
        zr_t = p.op("sync", lambda e, i=i, d=zrd: e.dma_start(
            d[:, 0:H * 128].rearrange("p (h q) -> p h q", q=128),
            zs_h.ap()[i].transpose([1, 0, 2])), wait=wait, inc=ZRs[par], incn=16)
        wait = [("DV", gp_x3[i - 2])] if i - 2 in gp_x3 else \
            [("PE", B0_PE), ("DV", B0_DV), ("SC", B0_SC), ("GP", B0_GP),
             (XAs[3], xa_tot[3])]
        xc_t[i] = p.op("sync", lambda e, i=i, par=par: e.dma_start(
            xin[par], x_h.ap()[i * 128:(i + 1) * 128, :]),
            wait=wait, inc=XCs[par], incn=16)
        for hh in range(H):
            for s in range(NPS):
                wait = []
                if hh == 0 and s == 0:
                    wait = [(ZRs[par], zr_t), ("WO", wo_tot),
                            ("SC", sc_s1.get(i - 1, B0_SC if i == 0 else 0)),
                            ("CONSTG", constg_t)]
                    if i == 0:
                        wait.append(("DV", B0_DV))
                p.op("tensor", lambda e, hh=hh, s=s, d=zrd: e.matmul(
                    projb[s], d[:, hh * 128:(hh + 1) * 128],
                    Bz[:, hh * DM + s * 512:hh * DM + s * 512 + PSW],
                    start=(hh == 0), stop=False), wait=wait)
        for s in range(NPS):
            tgt = p.op("tensor", lambda e, s=s: e.matmul(
                projb[s], oner_sb[:], bo_sb[0:1, s * 512:s * 512 + PSW],
                start=False, stop=True), inc="PE" if s == NPS - 1 else None)
        pe_proj[i] = tgt

    def emit_front_ln2(i):
        nonlocal sc_r2_prev
        par = i % 2
        wait = [("PE", pe_proj[i]), ("DV", dv_aggr2.get(i - 1, 0)),
                ("DV", gp_x2.get(i - 1, 0))]
        if i == 0:
            wait += [("DV", B0_DV), ("GP", B0_GP), ("ZW0", zw_tot[0]),
                     ("ZW1", zw_tot[1])]
        w1 = min(1024, DM)
        if NPS > 2:
            sc_a = p.op("scalar", lambda e, w1=w1: e.copy(
                s1[:, 0:w1], psA[:, 0:w1]), wait=wait, inc="SC")
            p.drain("scalar")
            sc_s1[i] = p.op("scalar", lambda e, w1=w1: e.copy(
                s1[:, w1:2 * w1], psB[:, 0:w1]), inc="SC")
        else:
            sc_a = sc_s1[i] = p.op("scalar", lambda e, w1=w1: e.copy(
                s1[:, 0:w1], psA[:, 0:w1]), wait=wait, inc="SC")
        for g in range(NST):
            hw = sc_a if (g * STW < w1) else sc_s1[i]
            w = [("SC", hw)]
            if g == 0:
                w.append(("SC", sc_r4_prev))
                if i == 0:
                    w.append(("GP", B_GP))
            p.op("vector", lambda e, g=g: e.bn_stats(
                st6[0][:, 6 * g:6 * g + 6], s1[:, STW * g:STW * (g + 1)]), wait=w)
        p.drain("vector")
        dv_aggr2[i] = p.op("vector", lambda e: e.bn_aggr(mv2[0][:], st6[0][:]),
                           inc="DV")
        sc_r2[i] = emit_rstd(0, dv_aggr2[i],
                             war_dv=max(dv_ct.get(i - 1, 0),
                                        gp_x2.get(i - 1, 0)))
        sc_r2_prev = sc_r2[i]
        p.drain("vector")
        dv_ct[i] = p.op("vector", lambda e, par=par: e.tensor_tensor(
            corr[:, par:par + 1], mv2[0][:, 0:1], rstd[0][:, 0:1], ALU.mult),
            wait=[("SC", sc_r2[i])], inc="DV")

    def emit_front_chain(i):
        nonlocal sc_r3_prev, dv_ln3_prev
        par = i % 2
        wait = [("SC", sc_r2[i]), (XCs[par], xc_t[i])]
        if i == 0:
            wait += [(FWs[0], fw_tot2[0]), (FWs[1], fw_tot2[1]),
                     ("DV", B_DV)]
        p.drain("vector")
        gp_x2[i] = p.op("vector", lambda e, par=par: e.scalar_tensor_tensor(
            xin[par], s1, rstd[0][:, 0:1], xin[par], ALU.mult, ALU.add),
            wait=wait, inc="DV")
        aggr3 = emit_ln_stats(xin[par], 1, [("DV", gp_x2[i])],
                              war_sc=sc_r3_prev)
        sc_r3 = emit_rstd(1, aggr3, war_dv=dv_ln3_prev)
        sc_r3_prev = sc_r3
        p.drain("vector")
        w3 = [("SC", sc_r3), ("PE", pe_tr3_d.get(i - 2, 0))]
        if i < 2:
            w3 += [("PE", B_PE), ("SC", B_SC)]
        dv_ln3[i] = p.op("vector", lambda e, par=par: e.tensor_scalar(
            ln3v[par], xin[par], mv2[1][:, 0:1], rstd[1][:, 0:1],
            ALU.subtract, ALU.mult), wait=w3, inc="DV")
        dv_ln3_prev = dv_ln3[i]

    def emit_back_tr_ffn(i):
        par = i % 2
        wait = [("DV", dv_ln3[i]),
                ("SC", sc_s2.get(i - 1, B_SC if i == 0 else 0))]
        if i == 0:
            wait.append(("DV", B_DV))
        pe_half = 0
        for c in range(NCH):
            w = wait if c == 0 else ()
            buf = psD_bf if c < hn else psE_bf
            cc = c % hn
            tgt = p.op("tensor", lambda e, c=c, buf=buf, cc=cc, par=par:
                       e.transpose(buf[:, cc * 128:(cc + 1) * 128],
                                   ln3v[par][:, c * 128:(c + 1) * 128],
                                   eye_sb[:]), wait=w,
                       inc="PE" if c in (hn - 1, NCH - 1) else None)
            if c == hn - 1:
                pe_half = tgt
        pe_tr3 = tgt
        pe_tr3_d[i] = pe_tr3
        dv_lo = p.op("vector", lambda e: e.tensor_copy(
            ln3t[:, 0:hn * 128], psD_bf[:, 0:hn * 128]),
            wait=[("PE", pe_half)], inc="DV")
        dv_ln3t[i] = p.op("vector", lambda e: e.tensor_copy(
            ln3t[:, hn * 128:NCH * 128], psE_bf[:, 0:(NCH - hn) * 128]),
            wait=[("PE", pe_tr3)], inc="DV") if NCH > hn else dv_lo
        for c in range(NCH):
            for s in range(NPS):
                wait = []
                if c == 0:
                    wait = [("DV", dv_ln3t[i] if s == NPS - 1 else dv_lo)]
                    if s == 0:
                        wait.append(
                            ("SC", sc_s2.get(i - 1, B_SC if i == 0 else 0)))
                        if i == 0:
                            wait.append(("DV", B_DV))
                p.op("tensor", lambda e, c=c, s=s: e.matmul(
                    ffnb[s], ln3t[:, c * 128:(c + 1) * 128],
                    bwt3[:, c, s * 512:s * 512 + PSW],
                    start=(c == 0), stop=False), wait=wait)
        for s in range(NPS):
            tgt = p.op("tensor", lambda e, s=s: e.matmul(
                ffnb[s], oner_sb[:], fb_sb[0:1, s * 512:s * 512 + PSW],
                start=False, stop=True), inc="PE" if s == NPS - 1 else None)
        pe_ffn[i] = tgt

    def emit_back_tail(i):
        nonlocal sc_r4_prev, out_prev
        par = i % 2
        wait = [("PE", pe_ffn[i]), ("DV", dv_ln3[i]),
                ("DV", dv_x3ts.get(i - 1, 0))]
        if i < 2:
            wait += [("PE", B_PE)] + [(f"WS{k}", ws_tot[k]) for k in range(4)]
        w1 = min(1024, DM)
        s2_copies = [(s2buf[:, 0:w1], psC[:, 0:w1])]
        if NPS > 2:
            s2_copies.append((s2buf[:, 1024:1536], psD[:]))
        if NPS > 3:
            s2_copies.append((s2buf[:, 1536:2048], psE[:]))
        last = None
        for ci, (d, srcp) in enumerate(s2_copies):
            if ci > 0:
                p.drain("scalar")
            last = p.op("scalar", lambda e, d=d, srcp=srcp: e.copy(d, srcp),
                        wait=(wait if ci == 0 else ()),
                        inc="SC" if ci == len(s2_copies) - 1 else None)
        sc_s2[i] = last
        aggr4 = emit_ln_stats(s2buf, 2, [("SC", sc_s2[i])], war_sc=sc_r4_prev)
        sc_r4 = emit_rstd(2, aggr4, war_dv=dv_x3ts.get(i - 1, 0))
        sc_r4_prev = sc_r4
        p.drain("vector")
        p.op("vector", lambda e: e.tensor_tensor(
            corr[:, 2:3], mv2[2][:, 0:1], rstd[2][:, 0:1], ALU.mult),
            wait=[("SC", sc_r4)])
        p.drain("vector")
        p.op("vector", lambda e, par=par: e.tensor_tensor(
            corr[:, 2:3], corr[:, 2:3], corr[:, par:par + 1], ALU.add))
        p.drain("vector")
        p.op("vector", lambda e: e.tensor_scalar(
            corr[:, 2:3], corr[:, 2:3], -1.0, None, ALU.mult))
        p.drain("vector")
        wait = [("OUT0", out_prev)]
        dv_x3ts[i] = p.op("vector", lambda e: e.tensor_scalar(
            x3buf, s2buf, rstd[2][:, 0:1], corr[:, 2:3],
            ALU.mult, ALU.add), wait=wait, inc="DV")
        p.drain("vector")
        gp_x3[i] = p.op("vector", lambda e, par=par: e.tensor_tensor(
            x3buf, x3buf, xin[par], ALU.add), inc="DV")
        out_prev = p.op("gpsimd", lambda e, i=i: e.dma_start(
            out_h.ap()[i * 128:(i + 1) * 128, :], x3buf),
            wait=[("DV", gp_x3[i])], inc="OUT0", incn=16)

    B0_PE = p.cnt("PE")
    B0_DV = p.cnt("DV")
    B0_SC = p.cnt("SC")
    B0_GP = p.cnt("GP")
    for t in range(NT):
        emit_bw_tile(t)
        if t == 2 and NT >= 12:
            emit_front_pe(0)
    emit_bw_evacs(NT - 1)

    B_PE = p.cnt("PE")
    B_DV = p.cnt("DV")
    B_SC = p.cnt("SC")
    B_GP = p.cnt("GP")
    fw_tot2 = [p.cnt("FW0"), p.cnt("FW1")]

    pre = 1 if NT >= 12 else 0
    for i in range(NT):
        if i >= pre:
            emit_front_pe(i)
        if i > 0:
            emit_back_tr_ffn(i - 1)
        emit_front_ln2(i)
        emit_front_chain(i)
        if i > 0:
            emit_back_tail(i - 1)
    emit_back_tr_ffn(NT - 1)
    emit_back_tail(NT - 1)

    p.op("sync", lambda e: e.nop(),
         wait=[("OUT0", p.cnt("OUT0"))])

    import contextlib
    with contextlib.ExitStack() as es:
        blk = es.enter_context(nc.Block())
        for eng in Plan.ENGS:
            getattr(blk, eng)(lambda e, eng=eng: [f(e) for f in p.plans[eng]])
    nc.finalize()
    return nc


_SEQ, _DM, _H, _B = 2048, 2048, 16, 8
_nc_cache = {}


def _get_nc():
    key = (_SEQ, _DM, _H)
    if key not in _nc_cache:
        _nc_cache[key] = build_program(*key)
    return _nc_cache[key]


def kernel(L0_attn_x, L0_attn_W_Q, L0_attn_W_K, L0_attn_W_V, L0_attn_W_O,
           L0_attn_b_Q, L0_attn_b_K, L0_attn_b_V, L0_attn_b_O,
           L0_ffn_weight, L0_ffn_bias):
    x = np.ascontiguousarray(np.asarray(L0_attn_x, dtype=np.float32))
    common = dict(
        wq=np.ascontiguousarray(np.asarray(L0_attn_W_Q, np.float32)),
        wk=np.ascontiguousarray(np.asarray(L0_attn_W_K, np.float32)),
        wv=np.ascontiguousarray(np.asarray(L0_attn_W_V, np.float32)),
        wo=np.ascontiguousarray(np.asarray(L0_attn_W_O, np.float32)),
        bqkvT=np.ascontiguousarray(np.concatenate(
            [np.asarray(L0_attn_b_Q, np.float32).T,
             np.asarray(L0_attn_b_K, np.float32).T,
             np.asarray(L0_attn_b_V, np.float32).T], axis=1)),
        bo=np.ascontiguousarray(np.asarray(L0_attn_b_O, np.float32)),
        fw=np.ascontiguousarray(np.asarray(L0_ffn_weight, np.float32)),
        fb=np.ascontiguousarray(np.asarray(L0_ffn_bias, np.float32)),
        eye=np.eye(128, dtype=BF),
        epsb=np.full((128, 1), 1e-5, np.float32),
        lncb=np.full((1, 1), math.log(1.0 / math.sqrt(128.0)), np.float32),
        tri=np.triu(np.ones((128, 128), np.float32)).astype(BF),
        onec=np.ones((128, 1), dtype=BF),
        oner=np.ones((1, 128), dtype=BF),
        onem=np.ones((128, 128), dtype=BF),
    )
    nc = _get_nc()
    in_maps = [dict(x=np.ascontiguousarray(x[b]), **common) for b in range(_B)]
    res = run_bass_kernel_spmd(nc, in_maps, core_ids=list(range(_B)))
    return np.stack([res.results[b]["out"] for b in range(_B)], axis=0)
